# version 01
# baseline (speedup 1.0000x reference)
import numpy as np
import ml_dtypes
from contextlib import ExitStack

import concourse.bass as bass
import concourse.mybir as mybir
from concourse.bass_utils import run_bass_kernel_spmd

F32 = mybir.dt.float32
BF16 = mybir.dt.bfloat16
I32 = mybir.dt.int32
AF = mybir.ActivationFunctionType
ALU = mybir.AluOpType
AX = mybir.AxisListType
BF = ml_dtypes.bfloat16

NCORES = 8
T = 2048
D = 2048
NSEQ = 2
NT = T // 128
NKD = D // 128
NTCH = 4

E0, H0 = 8, 1408
NO0 = H0 // 128
INTER = 5504
LN_EPS = 1e-5
SIM_MODE = False


class Sem:
    __slots__ = ("h", "val", "name")

    def __init__(self, h, name):
        self.h = h
        self.val = 0
        self.name = name


class Op:
    __slots__ = ("engine", "fn", "waits", "inc", "incn")

    def __init__(self, engine, fn, waits, inc, incn):
        self.engine = engine
        self.fn = fn
        self.waits = waits
        self.inc = inc
        self.incn = incn


class Plan:
    def __init__(self):
        self.ops = []

    def op(self, engine, fn, waits=(), inc=None):
        w = [(s, v) for (s, v) in waits if v > 0]
        self.ops.append(Op(engine, fn, w, inc, 1 if inc is not None else 0))
        if inc is not None:
            inc.val += 1
            return inc.val
        return None

    def wait_only(self, engine, waits):
        w = [(s, v) for (s, v) in waits if v > 0]
        self.ops.append(Op(engine, None, w, None, 0))

    def dma(self, queue, out, in_, sem, waits=(), **kw):
        w = [(s, v) for (s, v) in waits if v > 0]
        if SIM_MODE and sem.val > 0:
            w.append((sem, sem.val))

        def fn(eng, out=out, in_=in_, kw=kw):
            return eng.dma_start(out=out, in_=in_, **kw)

        self.ops.append(Op(queue, fn, w, sem, 16))
        sem.val += 16
        return sem.val


class Arena:
    def __init__(self, nc, size, ctx):
        self.nc = nc
        self.size = size
        self.backing = ctx.enter_context(
            nc.sbuf_tensor("arena", [128, size], mybir.dt.uint8))
        self.base = nc.lookup_mloc(self.backing).addr
        self.cur = 0
        self.n = 0

    def place(self, shape, dtype, offset, name=None):
        per_part = int(np.prod(shape[1:])) * mybir.dt.size(dtype)
        assert offset % 32 == 0, name
        assert offset + per_part <= self.size, (
            f"arena overflow: {name} off={offset} sz={per_part} cap={self.size}")
        self.n += 1
        name = "s_" + (name or f"ar{self.n}")
        hnd = self.nc.virtual_sbuf_tensor(name, list(shape), dtype)
        self.nc._tensor(hnd.name, list(hnd.shape), hnd.dtype, type="SB")
        mloc = self.nc.lookup_mloc(hnd)
        mloc.allocated = True
        mloc.addr = self.base + offset
        return hnd.ap()

    def alloc(self, shape, dtype, name=None):
        off = (self.cur + 31) & ~31
        ap = self.place(shape, dtype, off, name)
        self.cur = off + int(np.prod(shape[1:])) * mybir.dt.size(dtype)
        return ap


class K:
    def __init__(self, nc, ctx, arena_size=200 * 1024):
        self.nc = nc
        self.ctx = ctx
        self.P = Plan()
        self.ar = Arena(nc, arena_size, ctx)
        self.sem = {}
        for name in ("q_sync", "q_gp", "q_scl", "m_tns", "m_vec", "m_scl"):
            self.sem[name] = Sem(ctx.enter_context(nc.semaphore(name)), name)
        self.PF = [ctx.enter_context(nc.psum_tensor(f"pf{i}", [128, 512], F32)).ap()
                   for i in range(4)]
        self.PB = [ctx.enter_context(nc.psum_tensor(f"pb{i}", [128, 1024], BF16)).ap()
                   for i in range(2)]
        self.PST = ctx.enter_context(nc.psum_tensor("pst", [128, 512], F32)).ap()
        self.PSC = ctx.enter_context(nc.psum_tensor("psc", [128, 512], F32)).ap()
        self.C = {}
        self.din = {}

    def finalize(self):
        nc, plan = self.nc, self.P
        with nc.Block() as block:
            def run_engine(engname):
                def body(eng):
                    watermark = {}
                    for o in plan.ops:
                        if o.engine != engname:
                            continue
                        for (s, v) in o.waits:
                            if watermark.get(s.name, -1) < v:
                                eng.wait_ge(s.h, v)
                                watermark[s.name] = v
                        if o.fn is None:
                            continue
                        ins = o.fn(eng)
                        if o.inc is not None:
                            ins.then_inc(o.inc.h, o.incn)
                return body
            block.tensor(run_engine("tensor"))
            block.vector(run_engine("vector"))
            block.scalar(run_engine("scalar"))
            block.gpsimd(run_engine("gpsimd"))
            block.sync(run_engine("sync"))


def setup_consts(k):
    P, ar = k.P, k.ar
    C = k.C
    C["idn_bf"] = ar.alloc([128, 128], BF16, "idn_bf")
    C["idn_f32"] = ar.alloc([128, 128], F32, "idn_f32")
    C["ones_col"] = ar.alloc([128, 1], BF16, "ones_col")
    C["ones_row"] = ar.alloc([1, 512], BF16, "ones_row")
    C["stats"] = ar.alloc([128, 64], F32, "stats")
    C["bn"] = ar.alloc([128, 24], F32, "bn")
    q = k.sem["q_sync"]
    v = P.dma("sync", C["idn_bf"], k.din["idn_bf"], q)
    v = P.dma("sync", C["idn_f32"], k.din["idn_f32"], q)
    C["cdma"] = (q, v)
    C["eps_ln"] = ar.alloc([128, 1], F32, "eps_ln")
    mv = k.sem["m_vec"]
    P.op("vector", lambda e: e.memset(C["ones_col"], 1.0), inc=mv)
    P.op("vector", lambda e: e.memset(C["ones_row"], 1.0), inc=mv)
    vv = P.op("vector", lambda e: e.memset(C["eps_ln"], LN_EPS), inc=mv)
    C["cvec"] = (mv, vv)
    C["pst0"] = (mv, 0)


def emit_ln_to_xn(k, xin_rows, XN, L):
    P, C = k.P, k.C
    q = k.sem["q_sync"]
    mv, ms, mt = k.sem["m_vec"], k.sem["m_scl"], k.sem["m_tns"]
    stats, bn = C["stats"], C["bn"]
    xt, xnt = L["xt"], L["xnt"]
    xt_free = L["xt_free"]
    xnt_free = L["xnt_free"]
    pb_free = L["pb_free"]
    ev = 0
    for tt in range(NT):
        t0 = tt * 128
        xb = xt[tt % 2]
        ld = P.dma("sync", xb, xin_rows[t0:t0 + 128, :], q,
                   waits=[xt_free[tt % 2]])
        for c in range(4):
            vv = P.op("vector",
                      lambda e, xb=xb, c=c: e.bn_stats(
                          bn[:, c * 6:(c + 1) * 6], xb[:, c * 512:(c + 1) * 512]),
                      waits=[(q, ld)], inc=mv)
        vv = P.op("vector", lambda e: e.bn_aggr(stats[:, 0:2], bn[:, 0:24]),
                  waits=[(mv, vv)], inc=mv)
        sv = P.op("scalar", lambda e: e.activation(
            stats[:, 2:3], stats[:, 1:2], AF.Ln, bias=C["eps_ln"]),
            waits=[(mv, vv), C["cvec"]], inc=ms)
        sv = P.op("scalar", lambda e: e.activation(
            stats[:, 3:4], stats[:, 2:3], AF.Exp, scale=-0.5),
            waits=[(ms, sv)], inc=ms)
        xnb = xnt[tt % 2]
        av = P.op("vector",
                  lambda e, xb=xb, xnb=xnb: e.tensor_scalar(
                      xnb, xb, stats[:, 0:1], stats[:, 3:4],
                      ALU.subtract, ALU.mult),
                  waits=[(ms, sv), xnt_free[tt % 2]], inc=mv)
        xt_free[tt % 2] = (mv, av)
        for grp in range(4):
            pb = k.PB[grp % 2]
            tv = 0
            for j in range(4):
                kt = grp * 4 + j
                tv = P.op("tensor",
                          lambda e, xnb=xnb, kt=kt, j=j, pb=pb: e.transpose(
                              pb[:, j * 128:(j + 1) * 128],
                              xnb[:, kt * 128:(kt + 1) * 128], C["idn_bf"]),
                          waits=([(mv, av), pb_free[grp % 2], C["cdma"]]
                                 if j == 0 else ()),
                          inc=(mt if j == 3 else None))
            kt0 = grp * 4
            ev = P.op("vector",
                      lambda e, pb=pb, kt0=kt0, t0=t0: e.tensor_copy(
                          XN[:].rearrange("p (k t) -> p k t", k=NKD)
                            [:, kt0:kt0 + 4, t0:t0 + 128],
                          pb[:, 0:512].rearrange("p (k t) -> p k t", k=4)),
                      waits=[(mt, tv)], inc=mv)
            pb_free[grp % 2] = (mv, ev)
        xnt_free[tt % 2] = (mv, ev)
    return ev


def emit_l0ffn(k, xin_rows, xout_rows, W, L):
    P, C = k.P, k.C
    q, qg = k.sem["q_sync"], k.sem["q_gp"]
    mv, ms, mt = k.sem["m_vec"], k.sem["m_scl"], k.sem["m_tns"]
    XN = L["XN"]
    stats = C["stats"]

    P.wait_only("vector", [L["region_free_t"], L["region_free_s"]])
    P.wait_only("sync", [L["region_free_t"], L["region_free_v"],
                         L["region_free_s"]])

    xn_done = emit_ln_to_xn(k, xin_rows, XN, L)

    comb_tm, comb_fm8, comb_row = L["comb_tm"], L["comb_fm8"], L["comb_row"]
    sc, eq1, eq2, msk = L["sc"], L["eq1"], L["eq2"], L["msk"]
    pst_free = L["pst_free"]
    psc_free = L["psc_free"]
    gW = W["gateWT"]
    ev = 0
    for tt in range(NT):
        t0 = tt * 128
        for kt in range(NKD):
            P.op("tensor",
                 lambda e, kt=kt, t0=t0: e.matmul(
                     k.PSC[:, 0:8], XN[:, kt * T + t0:kt * T + t0 + 128],
                     gW[:, kt * 8:(kt + 1) * 8], start=(kt == 0), stop=False),
                 waits=([(mv, xn_done), W["small_ready"], psc_free, C["cvec"]]
                        if kt == 0 else ()))
        tv = P.op("tensor",
                  lambda e: e.matmul(k.PSC[:, 0:8], C["ones_row"][0:1, 0:128],
                                     W["gbrow"], start=False, stop=True),
                  inc=mt)
        vv = P.op("vector", lambda e: e.tensor_copy(sc, k.PSC[:, 0:8]),
                  waits=[(mt, tv)], inc=mv)
        psc_free = (mv, vv)
        vv = P.op("vector", lambda e: e.reduce_max(stats[:, 8:9], sc, axis=AX.X),
                  waits=[(mv, vv)], inc=mv)
        vv = P.op("vector", lambda e: e.tensor_scalar(
            eq1, sc, stats[:, 8:9], None, ALU.is_equal), waits=[(mv, vv)], inc=mv)
        vv = P.op("vector", lambda e: e.scalar_tensor_tensor(
            msk, eq1, -1e30, sc, ALU.mult, ALU.add), waits=[(mv, vv)], inc=mv)
        vv = P.op("vector", lambda e: e.reduce_max(stats[:, 9:10], msk, axis=AX.X),
                  waits=[(mv, vv)], inc=mv)
        vv = P.op("vector", lambda e: e.tensor_scalar(
            eq2, sc, stats[:, 9:10], None, ALU.is_equal), waits=[(mv, vv)], inc=mv)
        vv = P.op("vector", lambda e: e.tensor_tensor(
            stats[:, 10:11], stats[:, 8:9], stats[:, 9:10], ALU.subtract),
            waits=[(mv, vv)], inc=mv)
        sv = P.op("scalar", lambda e: e.activation(
            stats[:, 11:12], stats[:, 10:11], AF.Sigmoid),
            waits=[(mv, vv)], inc=ms)
        vv = P.op("vector", lambda e: e.tensor_scalar(
            stats[:, 12:13], stats[:, 11:12], 0.5, None, ALU.mult),
            waits=[(ms, sv)], inc=mv)
        vv = P.op("vector", lambda e: e.tensor_scalar(
            stats[:, 13:14], stats[:, 11:12], -0.5, 0.5, ALU.mult, ALU.add),
            waits=[(mv, vv)], inc=mv)
        ct = comb_tm[:, tt * 8:(tt + 1) * 8]
        vv = P.op("vector", lambda e, ct=ct: e.tensor_scalar(
            ct, eq1, stats[:, 12:13], None, ALU.mult), waits=[(mv, vv)], inc=mv)
        vv = P.op("vector", lambda e, ct=ct: e.scalar_tensor_tensor(
            ct, eq2, stats[:, 13:14], ct, ALU.mult, ALU.add),
            waits=[(mv, vv)], inc=mv)
        tv = P.op("tensor", lambda e, ct=ct: e.transpose(
            k.PST[0:8, 0:128], ct, C["idn_f32"]),
            waits=[(mv, vv), pst_free], inc=mt)
        ev = P.op("vector", lambda e, tt=tt: e.tensor_copy(
            comb_fm8[:, tt * 128:(tt + 1) * 128], k.PST[0:8, 0:128]),
            waits=[(mt, tv)], inc=mv)
        pst_free = (mv, ev)
    comb_done = ev

    hsc = W["hsc"]
    wslot, bcS, silu, hrow = L["wslot"], L["bcS"], L["silu"], L["hrow"]
    comb_row_free = (mt, 0)
    wslot_free = L["wslot_free"]
    hrow_free = L["hrow_free"]
    silu_free = L.setdefault("silu_free", [(mv, 0), (mv, 0)])
    pf_free = L["pfA_free"]
    dv = 0
    for e in range(E0):
        crv = P.dma("sync", comb_row, comb_fm8[e:e + 1, :], q,
                    waits=[(mv, comb_done), comb_row_free])
        bv = 0
        for tc in range(NTCH):
            tv = P.op("tensor",
                      lambda e_, e=e, tc=tc: e_.matmul(
                          k.PST[:, 0:512], C["ones_row"][0:1, 0:128],
                          comb_row[0:1, tc * 512:(tc + 1) * 512],
                          start=True, stop=True),
                      waits=[(q, crv), pst_free, C["cvec"]], inc=mt)
            bv = P.op("vector",
                      lambda e_, tc=tc: e_.tensor_copy(
                          bcS[:, tc * 512:(tc + 1) * 512], k.PST[:, 0:512]),
                      waits=[(mt, tv)], inc=mv)
            pst_free = (mv, bv)
        comb_row_free = (mt, tv)
        bc_ready = (mv, bv)
        for o in range(NO0):
            ws = wslot[o % 2]
            r0 = (e * NO0 + o) * 128
            wv = P.dma("sync", ws, W["w1bf"][r0:r0 + 128, :],
                       q, waits=[wslot_free[o % 2], W["w1_ready"][e]])
            hv = 0
            for tc in range(NTCH):
                pf = k.PF[tc % 2]
                tv = 0
                for kt in range(NKD):
                    tv = P.op("tensor",
                              lambda e_, ws=ws, kt=kt, tc=tc, pf=pf: e_.matmul(
                                  pf[:, 0:512], ws[:, kt * 128:(kt + 1) * 128],
                                  XN[:, kt * T + tc * 512:kt * T + (tc + 1) * 512],
                                  start=(kt == 0), stop=(kt == NKD - 1)),
                              waits=([(q, wv), pf_free[tc % 2], (mv, xn_done)]
                                     if kt == 0 else ()),
                              inc=(mt if kt == NKD - 1 else None))
                sb = silu[tc % 2]
                sv = P.op("scalar",
                          lambda e_, sb=sb, pf=pf, e=e, o=o: e_.activation(
                              sb, pf[:, 0:512], AF.Silu,
                              bias=W["b1sb"][:, o * 8 + e:o * 8 + e + 1]),
                          waits=[(mt, tv), W["small_ready"],
                                 silu_free[tc % 2]], inc=ms)
                pf_free[tc % 2] = (ms, sv)
                hv = P.op("vector",
                          lambda e_, sb=sb, tc=tc, o=o: e_.tensor_tensor(
                              hrow[o % 2][:, tc * 512:(tc + 1) * 512],
                              sb, bcS[:, tc * 512:(tc + 1) * 512], ALU.mult),
                          waits=[(ms, sv), bc_ready, hrow_free[o % 2]], inc=mv)
                silu_free[tc % 2] = (mv, hv)
            wslot_free[o % 2] = (mt, tv)
            r0h = (e * NO0 + o) * 128
            dv = P.dma("sync", hsc[r0h:r0h + 128, :], hrow[o % 2][:],
                       q, waits=[(mv, hv)])
            hrow_free[o % 2] = (q, dv)
    phaseA_done = (q, dv)
    L["pst_free"] = pst_free
    L["psc_free"] = psc_free

    W2SL, hslot, xs, xo = L["W2SL"], L["hslot"], L["xs"], L["xo"]
    barrier = [(mt, mt.val), (mv, mv.val), (ms, ms.val)]
    hslot_free = [barrier[0], barrier[0]]
    first_extra = barrier
    xs_free = L["xs_free"]
    xo_free = L["xo_free"]
    pfB_free = L["pfB_free"]
    w2sl_free = barrier[0]
    for dc in range(4):
        w2v = []
        for e in range(E0):
            r0 = (dc * E0 + e) * NO0 * 128
            v = P.dma("sync",
                      W2SL[:, e * NO0 * 512:(e + 1) * NO0 * 512].rearrange(
                          "p (r m) -> p r m", r=NO0),
                      W["w2bf"][r0:r0 + NO0 * 128, :].rearrange(
                          "(r p) m -> p r m", p=128),
                      q, waits=[w2sl_free, W["w2_ready"][dc]] + first_extra)
            w2v.append((q, v))
        first_extra = []
        for tt in range(NT):
            t0 = tt * 128
            hs = hslot[tt % 2]
            hv = P.dma("sync", hs[:].rearrange("p (r m) -> p r m", r=E0 * NO0),
                       hsc.rearrange("(r p) m -> p r m", p=128)[:, :, t0:t0 + 128],
                       q, waits=[hslot_free[tt % 2], phaseA_done])
            xsb = xs[tt % 3]
            xv = P.dma("sync", xsb,
                       xin_rows[t0:t0 + 128, dc * 512:(dc + 1) * 512],
                       q, waits=[xs_free[tt % 3]])
            pf = k.PF[2 + tt % 2]
            P.op("tensor",
                 lambda e_, tt=tt, dc=dc, pf=pf: e_.matmul(
                     pf[:, 0:512], comb_fm8[:, tt * 128:(tt + 1) * 128],
                     W["b2sb"][:, dc * 512:(dc + 1) * 512],
                     start=True, stop=False),
                 waits=[pfB_free[tt % 2], (mv, comb_done), (q, hv)])
            tv = 0
            for e in range(E0):
                for o in range(NO0):
                    i = e * NO0 + o
                    last = (i == E0 * NO0 - 1)
                    tv = P.op("tensor",
                              lambda e_, hs=hs, i=i, pf=pf, last=last: e_.matmul(
                                  pf[:, 0:512], hs[:, i * 128:(i + 1) * 128],
                                  W2SL[:, i * 512:(i + 1) * 512],
                                  start=False, stop=last),
                              waits=([w2v[e]] if (tt == 0 and o == 0) else ()),
                              inc=(mt if last else None))
            hslot_free[tt % 2] = (mt, tv)
            xob = xo[tt % 2]
            ev = P.op("vector",
                      lambda e_, xob=xob, pf=pf, xsb=xsb: e_.tensor_tensor(
                          xob, pf[:, 0:512], xsb, ALU.add),
                      waits=[(mt, tv), (q, xv), xo_free[tt % 2]], inc=mv)
            pfB_free[tt % 2] = (mv, ev)
            xs_free[tt % 3] = (mv, ev)
            ov = P.dma("sync", xout_rows[t0:t0 + 128, dc * 512:(dc + 1) * 512],
                       xob, q, waits=[(mv, ev)])
            xo_free[tt % 2] = (q, ov)
        w2sl_free = (mt, mt.val)
    L["region_free_t"] = (mt, mt.val)
    L["region_free_v"] = (mv, mv.val)
    L["region_free_s"] = (ms, ms.val)
    return (q, q.val)


def alloc_shared_bufs(k):
    ar = k.ar
    mv = k.sem["m_vec"]
    L = {}
    L["comb_tm"] = ar.alloc([128, NT * 8], F32, "comb_tm")
    L["comb_fm8"] = ar.alloc([8, T], BF16, "comb_fm8")
    L["comb_row"] = ar.alloc([1, T], BF16, "comb_row")
    for nm in ("sc", "eq1", "eq2", "msk"):
        L[nm] = ar.alloc([128, 8], F32, nm)
    L["xs"] = [ar.alloc([128, 512], F32, f"xs{i}") for i in range(3)]
    L["xo"] = [ar.alloc([128, 512], F32, f"xo{i}") for i in range(2)]
    L["silu"] = [ar.alloc([128, 512], BF16, f"silu{i}") for i in range(2)]
    L["bcS"] = ar.alloc([128, 2048], BF16, "bcS")
    ov1 = (ar.cur + 31) & ~31
    L["XN"] = ar.place([128, NKD * T], BF16, ov1, "XN")
    L["W2SL"] = ar.place([128, E0 * NO0 * 512], BF16, ov1, "W2SL")
    sz1 = max(NKD * T * 2, E0 * NO0 * 512 * 2)
    ov2 = ov1 + sz1
    off = ov2
    L["xt"] = []
    for i in range(2):
        L["xt"].append(ar.place([128, 2048], F32, off, f"xt{i}")); off += 8192
    L["xnt"] = []
    for i in range(2):
        L["xnt"].append(ar.place([128, 2048], BF16, off, f"xnt{i}")); off += 4096
    L["hrow"] = []
    for i in range(2):
        L["hrow"].append(ar.place([128, 2048], BF16, off, f"hrow{i}")); off += 4096
    L["wslot"] = []
    for i in range(2):
        L["wslot"].append(ar.place([128, 2048], BF16, off, f"wsl{i}")); off += 4096
    szA = off - ov2
    L["hslot"] = [ar.place([128, E0 * NO0 * 128], BF16,
                           ov2 + i * E0 * NO0 * 128 * 2, f"hslB{i}")
                  for i in range(2)]
    szB = 2 * E0 * NO0 * 128 * 2
    end = ov2 + max(szA, szB)
    assert end <= ar.size, f"arena overflow: {end}"
    ar.cur = end
    L["ov1"] = ov1
    L["ovsz"] = end - ov1
    L["xt_free"] = [(mv, 0), (mv, 0)]
    L["xnt_free"] = [(mv, 0), (mv, 0)]
    L["pb_free"] = [(mv, 0), (mv, 0)]
    L["pst_free"] = (mv, 0)
    L["psc_free"] = (mv, 0)
    L["wslot_free"] = [(mv, 0), (mv, 0)]
    L["hrow_free"] = [(mv, 0), (mv, 0)]
    L["pfA_free"] = [(mv, 0), (mv, 0)]
    L["pfB_free"] = [(mv, 0), (mv, 0)]
    L["xs_free"] = [(mv, 0)] * 3
    L["xo_free"] = [(mv, 0)] * 2
    L["region_free_t"] = (k.sem["m_tns"], 0)
    L["region_free_v"] = (mv, 0)
    L["region_free_s"] = (k.sem["m_scl"], 0)
    return L


def setup_l0ffn_weights(k):
    nc, P, ar = k.nc, k.P, k.ar
    qg = k.sem["q_gp"]
    q = k.sem["q_sync"]
    W = {}
    din = k.din
    din["w1t"] = nc.dram_tensor("w1t", [E0 * NO0 * 128, 2048], F32,
                                kind="ExternalInput").ap()
    din["w2t"] = nc.dram_tensor("w2t", [4 * E0 * NO0 * 128, 512], F32,
                                kind="ExternalInput").ap()
    din["gateWT"] = nc.dram_tensor("gateWT", [16 * 128, 8], F32,
                                   kind="ExternalInput").ap()
    din["gate_b"] = nc.dram_tensor("gate_b", [1, 8], F32,
                                   kind="ExternalInput").ap()
    din["gate_bias"] = nc.dram_tensor("gate_bias", [1, 8], F32,
                                      kind="ExternalInput").ap()
    din["b1sb"] = nc.dram_tensor("b1sb", [128, NO0 * 8], F32,
                                 kind="ExternalInput").ap()
    din["b2"] = nc.dram_tensor("b2", [8, 2048], F32, kind="ExternalInput").ap()
    W["w1bf"] = nc.dram_tensor("w1bf", [E0 * NO0 * 128, 2048], BF16,
                               kind="Internal").ap()
    W["w2bf"] = nc.dram_tensor("w2bf", [4 * E0 * NO0 * 128, 512], BF16,
                               kind="Internal").ap()
    W["hsc"] = nc.dram_tensor("hsc", [E0 * NO0 * 128, 2048], BF16,
                              kind="Internal").ap()
    W["w1_ready"] = []
    for e in range(E0):
        r0 = e * NO0 * 128
        v = P.dma("gpsimd", W["w1bf"][r0:r0 + NO0 * 128, :],
                  din["w1t"][r0:r0 + NO0 * 128, :], qg)
        W["w1_ready"].append((qg, v))
    W["w2_ready"] = []
    for dc in range(4):
        r0 = dc * E0 * NO0 * 128
        v = P.dma("gpsimd", W["w2bf"][r0:r0 + E0 * NO0 * 128, :],
                  din["w2t"][r0:r0 + E0 * NO0 * 128, :], qg)
        W["w2_ready"].append((qg, v))
    W["gateWT"] = ar.alloc([128, 16 * 8], BF16, "gateWTs")
    W["b1sb"] = ar.alloc([128, NO0 * 8], F32, "b1sbs")
    W["b2sb"] = ar.alloc([8, 2048], BF16, "b2sbs")
    gb0 = ar.alloc([1, 8], F32, "gb0")
    gb1 = ar.alloc([1, 8], F32, "gb1")
    W["gbrow"] = ar.alloc([1, 8], BF16, "gbrow")
    v = P.dma("gpsimd", W["gateWT"][:].rearrange("p (g e) -> p g e", g=16),
              din["gateWT"].rearrange("(g p) e -> p g e", p=128), qg)
    vb = P.dma("sync", W["b1sb"], din["b1sb"], q)
    v1 = P.dma("gpsimd", W["b2sb"], din["b2"], qg)
    v2 = P.dma("sync", gb0, din["gate_b"], q)
    v2 = P.dma("sync", gb1, din["gate_bias"], q)
    mv = k.sem["m_vec"]
    vv = P.op("vector", lambda e: e.tensor_tensor(gb0, gb0, gb1, ALU.add),
              waits=[(q, v2), (qg, v1)], inc=mv)
    vv = P.op("vector", lambda e: e.tensor_copy(W["gbrow"], gb0),
              waits=[(mv, vv)], inc=mv)
    W["small_ready"] = (mv, vv)
    return W


def prep_l0ffn_inputs(inp):
    w1 = np.asarray(inp["L0_ffn_w1"])
    w2 = np.asarray(inp["L0_ffn_w2"])
    d = {}
    blocks = []
    for e in range(E0):
        w1T = w1[e].T
        b = w1T.reshape(16, 128, NO0, 128).transpose(2, 1, 0, 3)
        blocks.append(b.reshape(NO0 * 128, 2048))
    d["w1t"] = np.ascontiguousarray(np.concatenate(blocks, 0), np.float32)
    out = np.empty((4, E0, NO0, 128, 512), np.float32)
    for e in range(E0):
        w2T = w2[e].T
        out[:, e] = w2T.reshape(NO0, 128, 4, 512).transpose(2, 0, 1, 3)
    d["w2t"] = out.reshape(4 * E0 * NO0 * 128, 512)
    gw = np.asarray(inp["L0_ffn_gate_w"])
    d["gateWT"] = np.ascontiguousarray(gw.T).reshape(16 * 128, 8)
    d["gate_b"] = np.asarray(inp["L0_ffn_gate_b"], np.float32).reshape(1, 8)
    d["gate_bias"] = np.asarray(inp["L0_ffn_bias"], np.float32).reshape(1, 8)
    b1 = np.asarray(inp["L0_ffn_b1"])
    d["b1sb"] = np.ascontiguousarray(
        b1.reshape(8, NO0, 128).transpose(2, 1, 0)).reshape(128, NO0 * 8)
    d["b2"] = np.asarray(inp["L0_ffn_b2"], np.float32)
    d["idn_bf"] = np.eye(128, dtype=BF)
    d["idn_f32"] = np.eye(128, dtype=np.float32)
    return d



NMASK = 4


def setup_l0attn_weights(k):
    nc, P, ar = k.nc, k.P, k.ar
    q, qg = k.sem["q_sync"], k.sem["q_gp"]
    W = {}
    din = k.din
    din["wqt"] = nc.dram_tensor("wqt", [16 * 128, 2048], F32,
                                kind="ExternalInput").ap()
    din["wkt"] = nc.dram_tensor("wkt", [4 * 128, 2048], F32,
                                kind="ExternalInput").ap()
    din["wvt"] = nc.dram_tensor("wvt", [2048, 512], F32,
                                kind="ExternalInput").ap()
    din["wot"] = nc.dram_tensor("wot", [4 * 16 * 128, 512], F32,
                                kind="ExternalInput").ap()
    din["bo"] = nc.dram_tensor("bo", [1, 2048], F32, kind="ExternalInput").ap()
    din["qn"] = nc.dram_tensor("qn", [128, 1], F32, kind="ExternalInput").ap()
    din["kn"] = nc.dram_tensor("kn", [128, 1], F32, kind="ExternalInput").ap()
    din["rope0"] = nc.dram_tensor("rope0", [128, 2 * 2048], BF16,
                                  kind="ExternalInput").ap()
    din["masks"] = nc.dram_tensor("masks", [128, NMASK * 512], BF16,
                                  kind="ExternalInput").ap()
    W["qsc"] = nc.dram_tensor("qsc", [16 * 128, 2048], BF16, kind="Internal").ap()
    W["ctxsc"] = nc.dram_tensor("ctxsc", [16 * 128, 2048], BF16,
                                kind="Internal").ap()
    W["qn"] = ar.alloc([128, 1], F32, "qn_s")
    W["kn"] = ar.alloc([128, 1], F32, "kn_s")
    W["rope0"] = ar.alloc([128, 2 * 2048], BF16, "rope0_s")
    W["masks"] = ar.alloc([128, NMASK * 512], BF16, "masks_s")
    W["bo"] = ar.alloc([1, 2048], BF16, "bo_s")
    v = P.dma("sync", W["qn"], din["qn"], q)
    v = P.dma("sync", W["kn"], din["kn"], q)
    v = P.dma("sync", W["rope0"], din["rope0"], q)
    v = P.dma("sync", W["masks"], din["masks"], q)
    sv = v
    v = P.dma("gpsimd", W["bo"], din["bo"], qg)
    W["ready"] = (q, sv)
    W["bo_ready"] = (qg, v)
    return W


def alloc_l0attn_bufs(k, L):
    ar = k.ar
    A = {}
    base = L["ov1"] + NKD * T * 2
    off = base
    A["KP"] = ar.place([128, 4 * 2048], BF16, off, "KP"); off += 4 * 2048 * 2
    A["V"] = ar.place([128, 16 * 512], BF16, off, "Vtm"); off += 16 * 512 * 2
    sub = off
    A["VW"] = ar.place([128, 16 * 512], BF16, sub, "VW")
    o = sub
    A["raw"] = ar.place([128, 2048], F32, o, "rawKQ"); o += 8192
    A["km"] = ar.place([128, 2048], BF16, o, "km"); o += 4096
    A["kms"] = ar.place([128, 2048], BF16, o, "kms"); o += 4096
    A["ropet"] = ar.place([128, 2048], BF16, o, "ropet"); o += 4096
    A["sqb"] = ar.place([128, 512], BF16, o, "sqb"); o += 1024
    A["srow"] = ar.place([1, 512], F32, o, "srow"); o += 2048
    A["rrow"] = ar.place([1, 512], F32, o, "rrow"); o += 2048
    A["rrowb"] = ar.place([1, 512], BF16, o, "rrowb"); o += 1024
    A["awsl"] = []
    for i in range(2):
        A["awsl"].append(ar.place([128, 2048], BF16, o, f"awsl{i}")); o += 4096
    A["qrow"] = [ar.alloc([128, 2048], BF16, f"qrow{i}") for i in range(2)]
    kq_end = o
    o = sub
    A["qslot"] = []
    for i in range(2):
        A["qslot"].append(ar.place([128, 2048], BF16, o, f"qslot{i}")); o += 4096
    A["Tb"] = ar.place([128, 2048], F32, o, "Tb"); o += 8192
    A["Pb"] = ar.place([128, 2048], BF16, o, "Pb"); o += 4096
    A["PTb"] = ar.place([128, 2048], BF16, o, "PTb"); o += 4096
    A["crow"] = []
    for i in range(2):
        A["crow"].append(ar.place([128, 2048], BF16, o, f"crow{i}")); o += 4096
    att_end = o
    o = sub
    A["prw"] = ar.place([128, 16 * 512], BF16, o, "prw"); o += 16 * 512 * 2
    A["prslot"] = []
    for i in range(2):
        A["prslot"].append(ar.place([128, 16 * 128], BF16, o, f"prslot{i}"))
        o += 16 * 128 * 2
    op_end = o
    end = max(kq_end, att_end, op_end, sub + 16 * 512 * 2)
    assert end <= L["ov1"] + L["ovsz"], f"attn overlay overflow {end}"
    return A


def emit_rms_chunk(k, A, raw_done, tc, nw_col, km):
    P, C = k.P, k.C
    mv, ms, mt = k.sem["m_vec"], k.sem["m_scl"], k.sem["m_tns"]
    raw = A["raw"][:, tc * 512:(tc + 1) * 512]
    sv = P.op("scalar", lambda e, raw=raw: e.activation(A["sqb"], raw, AF.Square),
              waits=[raw_done], inc=ms)
    tv = P.op("tensor", lambda e: e.matmul(
        k.PST[0:1, 0:512], C["ones_col"], A["sqb"], start=True, stop=True),
        waits=[(ms, sv), A["pst_free"], C["cvec"]], inc=mt)
    sv = P.op("scalar", lambda e: e.activation(
        A["srow"], k.PST[0:1, 0:512], AF.Ln, scale=1.0 / 128), waits=[(mt, tv)],
        inc=ms)
    sv = P.op("scalar", lambda e: e.activation(
        A["rrow"], A["srow"], AF.Exp, scale=0.5), waits=[(ms, sv)], inc=ms)
    vv = P.op("vector", lambda e: e.tensor_scalar(
        A["rrow"], A["rrow"], 1e-6, None, ALU.add), waits=[(ms, sv)], inc=mv)
    vv = P.op("vector", lambda e: e.reciprocal(A["rrow"], A["rrow"]),
              waits=[(mv, vv)], inc=mv)
    vv = P.op("vector", lambda e: e.tensor_copy(A["rrowb"], A["rrow"]),
              waits=[(mv, vv)], inc=mv)
    tv = P.op("tensor", lambda e: e.matmul(
        k.PST[:, 0:512], C["ones_row"][0:1, 0:128], A["rrowb"],
        start=True, stop=True), waits=[(mv, vv)], inc=mt)
    vv = P.op("vector", lambda e, raw=raw, tc=tc, nw_col=nw_col, km=km:
              e.scalar_tensor_tensor(
                  km[:, tc * 512:(tc + 1) * 512], raw, nw_col,
                  k.PST[:, 0:512], ALU.mult, ALU.mult),
              waits=[(mt, tv)], inc=mv)
    A["pst_free"] = (mv, vv)
    return vv


def emit_rope_head(k, A, rope_tab, km_done, out_full, half=64,
                   km=None, kms=None, rt=None):
    P = k.P
    q = k.sem["q_sync"]
    mv = k.sem["m_vec"]
    km = km if km is not None else A["km"]
    kms = kms if kms is not None else A["kms"]
    rt = rt if rt is not None else A["ropet"]
    d1 = P.dma("sync", kms[0:half, :], km[half:2 * half, :], q,
               waits=[(mv, km_done), A["kms_free"]])
    d1 = P.dma("sync", kms[half:2 * half, :], km[0:half, :], q)
    vv = P.op("vector", lambda e, km=km, rt=rt: e.tensor_tensor(
        rt, km, rope_tab[:, 0:T], ALU.mult),
        waits=[(mv, km_done), A["ropet_free"]], inc=mv)
    vv = P.op("vector", lambda e, kms=kms: e.tensor_tensor(
        kms, kms, rope_tab[:, T:2 * T], ALU.mult), waits=[(q, d1), (mv, vv)],
        inc=mv)
    vv = P.op("vector", lambda e, out_full=out_full, rt=rt, kms=kms:
              e.tensor_tensor(out_full, rt, kms, ALU.add),
              waits=[(mv, vv)], inc=mv)
    A["kms_free"] = (mv, vv)
    A["ropet_free"] = (mv, vv)
    return vv


ATT_PHASES = 5
ATT_DUMP = "XN"


def emit_l0attn(k, xin_rows, xout_rows, W, L, A):
    phases = ATT_PHASES
    P, C = k.P, k.C
    q, qg = k.sem["q_sync"], k.sem["q_gp"]
    mv, ms, mt = k.sem["m_vec"], k.sem["m_scl"], k.sem["m_tns"]
    XN = L["XN"]
    stats = C["stats"]
    SCL = 0.02 / np.sqrt(128.0)

    P.wait_only("vector", [L["region_free_t"], L["region_free_s"]])
    P.wait_only("sync", [L["region_free_t"], L["region_free_v"],
                         L["region_free_s"]])
    A["pst_free"] = L["pst_free"]
    A.setdefault("km_free", (mv, 0))
    A.setdefault("kms_free", (mv, 0))
    A.setdefault("ropet_free", (mv, 0))

    xn_done = emit_ln_to_xn(k, xin_rows, XN, L)

    wv = P.dma("gpsimd", A["VW"][:].rearrange("p (g m) -> p g m", g=16),
               k.din["wvt"].rearrange("(g p) m -> p g m", p=128), qg,
               waits=[(mv, xn_done), (mt, mt.val), (ms, ms.val)])
    pf_free = L["pfA_free"]
    for tt in range(NT):
        pf = k.PF[tt % 2]
        tv = 0
        for kt in range(NKD):
            tv = P.op("tensor", lambda e, kt=kt, tt=tt, pf=pf: e.matmul(
                pf[:, 0:512], XN[:, kt * T + tt * 128:kt * T + (tt + 1) * 128],
                A["VW"][:, kt * 512:(kt + 1) * 512],
                start=(kt == 0), stop=(kt == NKD - 1)),
                waits=([(qg, wv), pf_free[tt % 2], (mv, xn_done)]
                       if kt == 0 else ()),
                inc=(mt if kt == NKD - 1 else None))
        vv = P.op("vector", lambda e, tt=tt, pf=pf: e.tensor_copy(
            A["V"][:, tt * 512:(tt + 1) * 512], pf[:, 0:512]),
            waits=[(mt, tv)], inc=mv)
        pf_free[tt % 2] = (mv, vv)
    v_done = (mv, vv)

    if phases < 2:
        P.dma("gpsimd", k.dbg[:, 0:NKD * T], XN, qg, waits=[(mv, xn_done)])
        P.dma("gpsimd", k.dbg[:, 0:16 * 512], A["V"], qg, waits=[v_done]) \
            if ATT_DUMP == "V" else None
        dv = P.dma("gpsimd", xout_rows, xin_rows, qg, waits=[v_done])
        L["region_free_t"] = (mt, mt.val)
        L["region_free_v"] = (mv, mv.val)
        L["region_free_s"] = (ms, ms.val)
        return
    wslot = A["awsl"]
    wslot_free = A.setdefault("awsl_free", [(mv, 0), (mv, 0)])
    for g in range(4):
        ws = wslot[g % 2]
        wv = P.dma("gpsimd", ws, k.din["wkt"][g * 128:(g + 1) * 128, :], qg,
                   waits=[wslot_free[g % 2], W["ready"], v_done])
        for tc in range(NTCH):
            pf = k.PF[tc % 2]
            tv = 0
            for kt in range(NKD):
                tv = P.op("tensor", lambda e, ws=ws, kt=kt, tc=tc, pf=pf: e.matmul(
                    pf[:, 0:512], ws[:, kt * 128:(kt + 1) * 128],
                    XN[:, kt * T + tc * 512:kt * T + (tc + 1) * 512],
                    start=(kt == 0), stop=(kt == NKD - 1)),
                    waits=([(qg, wv), pf_free[tc % 2]] if kt == 0 else ()),
                    inc=(mt if kt == NKD - 1 else None))
            rv = P.op("vector", lambda e, tc=tc, pf=pf: e.tensor_copy(
                A["raw"][:, tc * 512:(tc + 1) * 512], pf[:, 0:512]),
                waits=[(mt, tv), A["km_free"]], inc=mv)
            pf_free[tc % 2] = (mv, rv)
            kmv = emit_rms_chunk(k, A, (mv, rv), tc, W["kn"], A["km"])
        wslot_free[g % 2] = (mt, tv)
        vv = emit_rope_head(k, A, W["rope0"], kmv,
                            A["KP"][:, g * 2048:(g + 1) * 2048])
        A["km_free"] = (mv, vv)
    k_done = (mv, vv)

    if phases < 3:
        P.dma("gpsimd", k.dbg[:, 0:4 * T], A["KP"], qg, waits=[k_done])
        dv = P.dma("gpsimd", xout_rows, xin_rows, qg, waits=[k_done])
        L["region_free_t"] = (mt, mt.val)
        L["region_free_v"] = (mv, mv.val)
        L["region_free_s"] = (ms, ms.val)
        return
    qrow_free = [(q, 0), (q, 0)]
    for h in range(16):
        ws = wslot[h % 2]
        wv = P.dma("gpsimd", ws, k.din["wqt"][h * 128:(h + 1) * 128, :], qg,
                   waits=[wslot_free[h % 2], W["ready"]])
        for tc in range(NTCH):
            pf = k.PF[tc % 2]
            tv = 0
            for kt in range(NKD):
                tv = P.op("tensor", lambda e, ws=ws, kt=kt, tc=tc, pf=pf: e.matmul(
                    pf[:, 0:512], ws[:, kt * 128:(kt + 1) * 128],
                    XN[:, kt * T + tc * 512:kt * T + (tc + 1) * 512],
                    start=(kt == 0), stop=(kt == NKD - 1)),
                    waits=([(qg, wv), pf_free[tc % 2]] if kt == 0 else ()),
                    inc=(mt if kt == NKD - 1 else None))
            rv = P.op("vector", lambda e, tc=tc, pf=pf: e.tensor_copy(
                A["raw"][:, tc * 512:(tc + 1) * 512], pf[:, 0:512]),
                waits=[(mt, tv), A["km_free"]], inc=mv)
            pf_free[tc % 2] = (mv, rv)
            kmv = emit_rms_chunk(k, A, (mv, rv), tc, W["qn"], A["km"])
        wslot_free[h % 2] = (mt, tv)
        vv = emit_rope_head(k, A, W["rope0"], kmv, A["qrow"][h % 2][:],
                            )
        A["km_free"] = (mv, vv)
        dv = P.dma("sync", W["qsc"][h * 128:(h + 1) * 128, :],
                   A["qrow"][h % 2][:], q, waits=[(mv, vv), qrow_free[h % 2]])
        qrow_free[h % 2] = (q, dv)
    q_done = (q, dv)

    if phases < 4:
        P.dma("gpsimd", k.dbg.rearrange("p (r m) -> p r m", r=16),
              W["qsc"].rearrange("(r p) m -> p r m", p=128), qg,
              waits=[q_done])
        dv = P.dma("gpsimd", xout_rows, xin_rows, qg, waits=[q_done])
        L["region_free_t"] = (mt, mt.val)
        L["region_free_v"] = (mv, mv.val)
        L["region_free_s"] = (ms, ms.val)
        return
    att_bar = [(mt, mt.val), (mv, mv.val), (ms, ms.val)]
    qslot_free = [att_bar[0], att_bar[0]]
    crow_free = [(q, 0), (q, 0)]
    pfS_free = [att_bar[1]] * 4
    pb_free = L["pb_free"]
    psc_free = L["psc_free"]
    first = att_bar
    for h in range(16):
        g = h // 4
        qs = A["qslot"][h % 2]
        qv = P.dma("sync", qs, W["qsc"][h * 128:(h + 1) * 128, :], q,
                   waits=[qslot_free[h % 2], q_done] + first)
        first = []
        for qt in range(NT):
            nch = qt // 4 + 1
            stv = []
            for cc in range(nch):
                tv = P.op("tensor", lambda e, qs=qs, qt=qt, cc=cc, g=g: e.matmul(
                    k.PF[cc][:, 0:512], qs[:, qt * 128:(qt + 1) * 128],
                    A["KP"][:, g * 2048 + cc * 512:g * 2048 + (cc + 1) * 512],
                    start=True, stop=True),
                    waits=[(q, qv), pfS_free[cc], k_done], inc=mt)
                stv.append(tv)
            mvv = P.op("vector", lambda e, qt=qt, nch=nch: e.tensor_tensor(
                k.PF[nch - 1][:, 0:512], k.PF[nch - 1][:, 0:512],
                W["masks"][:, (qt % 4) * 512:(qt % 4 + 1) * 512], ALU.add),
                waits=[(mt, stv[-1]), W["ready"]], inc=mv)
            tnv = []
            for cc in range(nch):
                wt = [(mt, stv[cc])] if cc != nch - 1 else [(mv, mvv)]
                sv = P.op("scalar", lambda e, cc=cc: e.activation(
                    A["Tb"][:, cc * 512:(cc + 1) * 512], k.PF[cc][:, 0:512],
                    AF.Tanh, scale=SCL), waits=wt, inc=ms)
                tnv.append(sv)
                pfS_free[cc] = (ms, sv)
            vv = P.op("vector", lambda e: e.reduce_max(
                stats[:, 16:17], A["Tb"][:, 0:512], axis=AX.X),
                waits=[(ms, tnv[0])], inc=mv)
            for cc in range(1, nch):
                vv = P.op("vector", lambda e, cc=cc: e.reduce_max(
                    stats[:, 17:18], A["Tb"][:, cc * 512:(cc + 1) * 512],
                    axis=AX.X), waits=[(ms, tnv[cc]), (mv, vv)], inc=mv)
                vv = P.op("vector", lambda e: e.tensor_tensor(
                    stats[:, 16:17], stats[:, 16:17], stats[:, 17:18], ALU.max),
                    waits=[(mv, vv)], inc=mv)
            vv = P.op("vector", lambda e: e.tensor_scalar(
                stats[:, 18:19], stats[:, 16:17], -50.0, None, ALU.mult),
                waits=[(mv, vv)], inc=mv)
            exv = []
            for cc in range(nch):
                sv = P.op("scalar", lambda e, cc=cc: e.activation(
                    A["Pb"][:, cc * 512:(cc + 1) * 512],
                    A["Tb"][:, cc * 512:(cc + 1) * 512], AF.Exp, scale=50.0,
                    bias=stats[:, 18:19],
                    accum_out=stats[:, 20 + cc:21 + cc]), waits=[(mv, vv)], inc=ms)
                exv.append(sv)
            vv = P.op("vector", lambda e: e.tensor_copy(
                stats[:, 24:25], stats[:, 20:21]), waits=[(ms, exv[0])], inc=mv)
            for cc in range(1, nch):
                vv = P.op("vector", lambda e, cc=cc: e.tensor_tensor(
                    stats[:, 24:25], stats[:, 24:25], stats[:, 20 + cc:21 + cc],
                    ALU.add), waits=[(ms, exv[cc]), (mv, vv)], inc=mv)
            vv = P.op("vector", lambda e: e.reciprocal(
                stats[:, 25:26], stats[:, 24:25]), waits=[(mv, vv)], inc=mv)
            scv = []
            for cc in range(nch):
                v2 = P.op("vector", lambda e, cc=cc: e.tensor_scalar(
                    A["Pb"][:, cc * 512:(cc + 1) * 512],
                    A["Pb"][:, cc * 512:(cc + 1) * 512],
                    stats[:, 25:26], None, ALU.mult), waits=[(mv, vv)], inc=mv)
                scv.append(v2)
            nkt = nch * 4
            ptv = []
            for grp in range((nkt + 3) // 4):
                pb = k.PB[grp % 2]
                tv = 0
                for j in range(4):
                    kt = grp * 4 + j
                    if kt >= nkt:
                        break
                    tv = P.op("tensor", lambda e, kt=kt, j=j, pb=pb: e.transpose(
                        pb[:, j * 128:(j + 1) * 128],
                        A["Pb"][:, kt * 128:(kt + 1) * 128], C["idn_bf"]),
                        waits=([(mv, scv[-1]), pb_free[grp % 2]]
                               if j == 0 else ()),
                        inc=(mt if (j == 3 or kt == nkt - 1) else None))
                nb = min(4, nkt - grp * 4)
                ev = P.op("vector", lambda e, pb=pb, grp=grp, nb=nb: e.tensor_copy(
                    A["PTb"][:, grp * 512:grp * 512 + nb * 128],
                    pb[:, 0:nb * 128]), waits=[(mt, tv)], inc=mv)
                pb_free[grp % 2] = (mv, ev)
                ptv.append(ev)
            tv = 0
            for kt in range(nkt):
                tv = P.op("tensor", lambda e, kt=kt, g=g, nkt=nkt: e.matmul(
                    k.PSC[:, 0:128],
                    A["V"][:, kt * 512 + g * 128:kt * 512 + (g + 1) * 128],
                    A["PTb"][:, kt * 128:(kt + 1) * 128],
                    start=(kt == 0), stop=(kt == nkt - 1)),
                    waits=([(mv, ptv[-1]), psc_free, v_done] if kt == 0 else ()),
                    inc=(mt if kt == nkt - 1 else None))
            cv = P.op("vector", lambda e, qt=qt, h=h: e.tensor_copy(
                A["crow"][h % 2][:, qt * 128:(qt + 1) * 128], k.PSC[:, 0:128]),
                waits=[(mt, tv), crow_free[h % 2]], inc=mv)
            psc_free = (mv, cv)
        dv = P.dma("sync", W["ctxsc"][h * 128:(h + 1) * 128, :],
                   A["crow"][h % 2][:], q, waits=[(mv, cv)])
        crow_free[h % 2] = (q, dv)
    ctx_done = (q, dv)
    L["psc_free"] = psc_free

    if phases < 5:
        P.dma("gpsimd", k.dbg.rearrange("p (r m) -> p r m", r=16),
              W["ctxsc"].rearrange("(r p) m -> p r m", p=128), qg,
              waits=[ctx_done])
        dv = P.dma("gpsimd", xout_rows, xin_rows, qg, waits=[ctx_done])
        L["region_free_t"] = (mt, mt.val)
        L["region_free_v"] = (mv, mv.val)
        L["region_free_s"] = (ms, ms.val)
        return
    op_bar = [(mt, mt.val), (mv, mv.val), (ms, ms.val)]
    prslot_free = [op_bar[0], op_bar[0]]
    xs, xo = L["xs"], L["xo"]
    xs_free, xo_free, pfB_free = L["xs_free"], L["xo_free"], L["pfB_free"]
    first = op_bar
    prw_free = op_bar[0]
    for dc in range(4):
        wv = P.dma("gpsimd", A["prw"][:].rearrange("p (r m) -> p r m", r=16),
                   k.din["wot"][dc * 16 * 128:(dc + 1) * 16 * 128, :].rearrange(
                       "(r p) m -> p r m", p=128),
                   qg, waits=[prw_free] + first)
        first = []
        for tt in range(NT):
            t0 = tt * 128
            ps = A["prslot"][tt % 2]
            pv = P.dma("sync", ps[:].rearrange("p (r m) -> p r m", r=16),
                       W["ctxsc"].rearrange("(r p) m -> p r m", p=128)
                       [:, :, t0:t0 + 128],
                       q, waits=[prslot_free[tt % 2], ctx_done])
            xsb = xs[tt % 3]
            xv = P.dma("sync", xsb,
                       xin_rows[t0:t0 + 128, dc * 512:(dc + 1) * 512],
                       q, waits=[xs_free[tt % 3]])
            pf = k.PF[2 + tt % 2]
            P.op("tensor", lambda e, pf=pf, dc=dc: e.matmul(
                pf[:, 0:512], C["ones_row"][0:1, 0:128],
                W["bo"][0:1, dc * 512:(dc + 1) * 512], start=True, stop=False),
                waits=[pfB_free[tt % 2], W["bo_ready"], (qg, wv), (q, pv)])
            tv = 0
            for ct in range(16):
                tv = P.op("tensor", lambda e, ps=ps, ct=ct, pf=pf: e.matmul(
                    pf[:, 0:512], ps[:, ct * 128:(ct + 1) * 128],
                    A["prw"][:, ct * 512:(ct + 1) * 512],
                    start=False, stop=(ct == 15)),
                    inc=(mt if ct == 15 else None))
            prslot_free[tt % 2] = (mt, tv)
            xob = xo[tt % 2]
            ev = P.op("vector", lambda e, xob=xob, pf=pf, xsb=xsb: e.tensor_tensor(
                xob, pf[:, 0:512], xsb, ALU.add),
                waits=[(mt, tv), (q, xv), xo_free[tt % 2]], inc=mv)
            pfB_free[tt % 2] = (mv, ev)
            xs_free[tt % 3] = (mv, ev)
            ov = P.dma("sync", xout_rows[t0:t0 + 128, dc * 512:(dc + 1) * 512],
                       xob, q, waits=[(mv, ev)])
            xo_free[tt % 2] = (q, ov)
        prw_free = (mt, mt.val)
    L["pst_free"] = A["pst_free"]
    L["pfA_free"][0] = (ms, ms.val)
    L["pfA_free"][1] = (ms, ms.val)
    L["pfB_free"][0] = (mv, mv.val)
    L["pfB_free"][1] = (mv, mv.val)
    L["region_free_t"] = (mt, mt.val)
    L["region_free_v"] = (mv, mv.val)
    L["region_free_s"] = (ms, ms.val)


def prep_l0attn_inputs(inp):
    d = {}
    Wq = np.asarray(inp["L0_attn_Wq"])
    Wk = np.asarray(inp["L0_attn_Wk"])
    Wv = np.asarray(inp["L0_attn_Wv"])
    Wo = np.asarray(inp["L0_attn_Wo"])
    d["wqt"] = np.ascontiguousarray(
        Wq.reshape(16, 128, 16, 128).transpose(0, 3, 2, 1)).reshape(2048, 2048)
    d["wkt"] = np.ascontiguousarray(
        Wk.reshape(4, 128, 16, 128).transpose(0, 3, 2, 1)).reshape(512, 2048)
    d["wvt"] = np.ascontiguousarray(Wv.T)
    WoT = Wo.T
    d["wot"] = np.ascontiguousarray(
        WoT.reshape(16, 128, 4, 512).transpose(2, 0, 1, 3)).reshape(4 * 16 * 128, 512)
    d["bo"] = np.asarray(inp["L0_attn_bo"], np.float32).reshape(1, 2048)
    d["qn"] = np.asarray(inp["L0_attn_q_norm_w"], np.float32).reshape(128, 1)
    d["kn"] = np.asarray(inp["L0_attn_k_norm_w"], np.float32).reshape(128, 1)
    inv = 1.0 / 10000.0 ** (np.arange(0, 128, 2) / 128.0)
    ang = np.outer(np.arange(T), inv)
    cosT, sinT = np.cos(ang).T, np.sin(ang).T
    Cf = np.concatenate([cosT, cosT], 0)
    Sx = np.concatenate([sinT, -sinT], 0)
    d["rope0"] = np.concatenate([Cf, Sx], 1).astype(BF)
    m = np.zeros((128, NMASK, 512), np.float32)
    for v in range(NMASK):
        r = np.arange(128)[:, None]
        kk = np.arange(512)[None, :]
        m[:, v, :] = np.where(kk <= v * 128 + r, 0.0, -1e9)
    d["masks"] = m.reshape(128, NMASK * 512).astype(BF)
    return d


NO1 = 43


def setup_l1ffn_weights(k):
    nc, P, ar = k.nc, k.P, k.ar
    q, qg = k.sem["q_sync"], k.sem["q_gp"]
    W = {}
    din = k.din
    din["gidxT"] = nc.dram_tensor("gidxT", [2048, INTER], I32,
                                  kind="ExternalInput").ap()
    din["uidxT"] = nc.dram_tensor("uidxT", [2048, INTER], I32,
                                  kind="ExternalInput").ap()
    din["didxT"] = nc.dram_tensor("didxT", [INTER, 2048], I32,
                                  kind="ExternalInput").ap()
    din["lut"] = nc.dram_tensor("lut", [1, 16], F32, kind="ExternalInput").ap()
    W["wg"] = nc.dram_tensor("wg_sc", [NO1 * 128, 2048], BF16, kind="Internal").ap()
    W["wu"] = nc.dram_tensor("wu_sc", [NO1 * 128, 2048], BF16, kind="Internal").ap()
    W["wd"] = nc.dram_tensor("wd_sc", [4 * NO1 * 128, 512], BF16,
                             kind="Internal").ap()
    W["hsc1"] = nc.dram_tensor("hsc1", [NO1 * 128, 2048], BF16,
                               kind="Internal").ap()
    W["lutb"] = ar.alloc([1, 16], BF16, "lutb")
    W["lutc"] = ar.alloc([128, 16], F32, "lutc")
    v = P.dma("gpsimd", W["lutb"], din["lut"], qg)
    mv, mt = k.sem["m_vec"], k.sem["m_tns"]
    tv = P.op("tensor", lambda e: e.matmul(
        k.PST[:, 0:16], k.C["ones_row"][0:1, 0:128], W["lutb"],
        start=True, stop=True), waits=[(qg, v), k.C["cvec"], k.C["pst0"]], inc=mt)
    vv = P.op("vector", lambda e: e.tensor_copy(W["lutc"], k.PST[:, 0:16]),
              waits=[(mt, tv)], inc=mv)
    k.C["pst0"] = (mv, vv)
    W["lut_ready"] = (mv, vv)
    return W


def emit_ghost_mat(k, W, G):
    P, C = k.P, k.C
    q, qg = k.sem["q_sync"], k.sem["q_gp"]
    mv, mt = k.sem["m_vec"], k.sem["m_tns"]
    lutc = W["lutc"]
    ids, eqt, acc = G["ids"], G["eqt"], G["acc"]
    ids_free = [(mv, 0), (mv, 0)]
    acc_free = [(q, 0), (q, 0)]
    eqt_free = [(mv, 0)]
    din = k.din

    def slab(src_rows, n_free, out_ap, slot):
        iv = P.dma("gpsimd", ids[slot][:, 0:n_free], src_rows, qg,
                   waits=[ids_free[slot]])
        vv = 0
        for e2 in range(16):
            vv = P.op("vector",
                      lambda e_, e2=e2, slot=slot, n_free=n_free: e_.tensor_scalar(
                          eqt[:, 0:n_free], ids[slot][:, 0:n_free], float(e2),
                          lutc[:, e2:e2 + 1], ALU.is_equal, ALU.mult),
                      waits=([(qg, iv), W["lut_ready"], eqt_free[0]]
                             if e2 == 0 else [(mv, vv)]),
                      inc=mv)
            if e2 == 0:
                vv = P.op("vector",
                          lambda e_, slot=slot, n_free=n_free: e_.tensor_copy(
                              acc[slot][:, 0:n_free], eqt[:, 0:n_free]),
                          waits=[(mv, vv), acc_free[slot]], inc=mv)
            else:
                vv = P.op("vector",
                          lambda e_, slot=slot, n_free=n_free: e_.tensor_tensor(
                              acc[slot][:, 0:n_free], acc[slot][:, 0:n_free],
                              eqt[:, 0:n_free], ALU.add),
                          waits=[(mv, vv)], inc=mv)
        ids_free[slot] = (mv, vv)
        eqt_free[0] = (mv, vv)
        dv = P.dma("sync", out_ap, G["acc_view"](slot), q, waits=[(mv, vv)])
        acc_free[slot] = (q, dv)
        return dv

    for mat, srcnm in ((W["wg"], "gidxT"), (W["wu"], "uidxT")):
        for ht in range(16):
            out_ap = mat.rearrange("(o p) (h m) -> h p o m", p=128, m=128)[ht]
            G["acc_view"] = lambda slot: G["acc"][slot][:, 0:INTER].rearrange(
                "p (o m) -> p o m", o=NO1)
            dv = slab(din[srcnm][ht * 128:(ht + 1) * 128, :], INTER,
                      out_ap, ht % 2)
    for ot in range(NO1):
        out_ap = W["wd"].rearrange("(d o p) m -> o p d m", o=NO1, p=128)[ot]
        G["acc_view"] = lambda slot: G["acc"][slot][:, 0:2048].rearrange(
            "p (d m) -> p d m", d=4)
        dv = slab(din["didxT"][ot * 128:(ot + 1) * 128, :], 2048,
                  out_ap, ot % 2)
    return (q, dv)


def emit_l1ffn(k, xin_rows, xout_rows, W, L, mat_done):
    P, C = k.P, k.C
    q, qg = k.sem["q_sync"], k.sem["q_gp"]
    mv, ms, mt = k.sem["m_vec"], k.sem["m_scl"], k.sem["m_tns"]
    XN = L["XN"]

    P.wait_only("vector", [L["region_free_t"], L["region_free_s"]])
    P.wait_only("sync", [L["region_free_t"], L["region_free_v"],
                         L["region_free_s"]])
    xn_done = emit_ln_to_xn(k, xin_rows, XN, L)

    wslot = L["wslot"]
    hrow = L["hrow"]
    uslot = L["uslot"]
    silu = L["silu"]
    wslot_free = L["wslot_free"]
    uslot_free = L["uslot_free"]
    hrow_free = L["hrow_free"]
    silu_free = L["silu_free"]
    pf_free = L["pfA_free"]
    pfu_free = L["pfB_free"]
    dv = 0
    for o in range(NO1):
        ws, us = wslot[o % 2], uslot[o % 2]
        r0 = o * 128
        wv = P.dma("sync", ws, W["wg"][r0:r0 + 128, :], q,
                   waits=[wslot_free[o % 2], mat_done])
        uv = P.dma("sync", us, W["wu"][r0:r0 + 128, :], q,
                   waits=[uslot_free[o % 2]])
        hv = 0
        for tc in range(NTCH):
            pf = k.PF[tc % 2]
            pu = k.PF[2 + tc % 2]
            tv = 0
            for kt in range(NKD):
                tv = P.op("tensor",
                          lambda e_, ws=ws, kt=kt, tc=tc, pf=pf: e_.matmul(
                              pf[:, 0:512], ws[:, kt * 128:(kt + 1) * 128],
                              XN[:, kt * T + tc * 512:kt * T + (tc + 1) * 512],
                              start=(kt == 0), stop=(kt == NKD - 1)),
                          waits=([(q, wv), pf_free[tc % 2], (mv, xn_done)]
                                 if kt == 0 else ()),
                          inc=(mt if kt == NKD - 1 else None))
            tvu = 0
            for kt in range(NKD):
                tvu = P.op("tensor",
                           lambda e_, us=us, kt=kt, tc=tc, pu=pu: e_.matmul(
                               pu[:, 0:512], us[:, kt * 128:(kt + 1) * 128],
                               XN[:, kt * T + tc * 512:kt * T + (tc + 1) * 512],
                               start=(kt == 0), stop=(kt == NKD - 1)),
                           waits=([(q, uv), pfu_free[tc % 2]] if kt == 0 else ()),
                           inc=(mt if kt == NKD - 1 else None))
            sb = silu[tc % 2]
            sv = P.op("scalar",
                      lambda e_, sb=sb, pf=pf: e_.activation(
                          sb, pf[:, 0:512], AF.Silu),
                      waits=[(mt, tv), silu_free[tc % 2]], inc=ms)
            pf_free[tc % 2] = (ms, sv)
            hv = P.op("vector",
                      lambda e_, sb=sb, tc=tc, o=o, pu=pu: e_.tensor_tensor(
                          hrow[o % 2][:, tc * 512:(tc + 1) * 512],
                          sb, pu[:, 0:512], ALU.mult),
                      waits=[(ms, sv), (mt, tvu), hrow_free[o % 2]], inc=mv)
            silu_free[tc % 2] = (mv, hv)
            pfu_free[tc % 2] = (mv, hv)
        wslot_free[o % 2] = (mt, tv)
        uslot_free[o % 2] = (mt, tvu)
        r0h = o * 128
        dv = P.dma("sync", W["hsc1"][r0h:r0h + 128, :], hrow[o % 2][:],
                   q, waits=[(mv, hv)])
        hrow_free[o % 2] = (q, dv)
    phaseA_done = (q, dv)

    WDSL, hslot, xs, xo = L["W2SL"], L["hslot"], L["xs"], L["xo"]
    barrier = [(mt, mt.val), (mv, mv.val), (ms, ms.val)]
    hslot_free = [barrier[0], barrier[0]]
    first_extra = barrier
    xs_free = L["xs_free"]
    xo_free = L["xo_free"]
    pfB_free = L["pfB_free"]
    wdsl_free = barrier[0]
    for dc in range(4):
        r0 = dc * NO1 * 128
        w2v = P.dma("sync",
                    WDSL[:, 0:NO1 * 512].rearrange("p (r m) -> p r m", r=NO1),
                    W["wd"][r0:r0 + NO1 * 128, :].rearrange(
                        "(r p) m -> p r m", p=128),
                    q, waits=[wdsl_free, mat_done] + first_extra)
        first_extra = []
        for tt in range(NT):
            t0 = tt * 128
            hs = hslot[tt % 2]
            hv = P.dma("sync", hs[:, 0:NO1 * 128].rearrange(
                           "p (r m) -> p r m", r=NO1),
                       W["hsc1"].rearrange("(r p) m -> p r m", p=128)
                       [:, :, t0:t0 + 128],
                       q, waits=[hslot_free[tt % 2], phaseA_done])
            xsb = xs[tt % 3]
            xv = P.dma("sync", xsb,
                       xin_rows[t0:t0 + 128, dc * 512:(dc + 1) * 512],
                       q, waits=[xs_free[tt % 3]])
            pf = k.PF[2 + tt % 2]
            tv = 0
            for o in range(NO1):
                tv = P.op("tensor",
                          lambda e_, hs=hs, o=o, pf=pf: e_.matmul(
                              pf[:, 0:512], hs[:, o * 128:(o + 1) * 128],
                              WDSL[:, o * 512:(o + 1) * 512],
                              start=(o == 0), stop=(o == NO1 - 1)),
                          waits=([pfB_free[tt % 2], (q, w2v), (q, hv)]
                                 if o == 0 else ()),
                          inc=(mt if o == NO1 - 1 else None))
            hslot_free[tt % 2] = (mt, tv)
            xob = xo[tt % 2]
            ev = P.op("vector",
                      lambda e_, xob=xob, pf=pf, xsb=xsb: e_.scalar_tensor_tensor(
                          xob, pf[:, 0:512], 0.5, xsb, ALU.mult, ALU.add),
                      waits=[(mt, tv), (q, xv), xo_free[tt % 2]], inc=mv)
            pfB_free[tt % 2] = (mv, ev)
            xs_free[tt % 3] = (mv, ev)
            ov = P.dma("sync", xout_rows[t0:t0 + 128, dc * 512:(dc + 1) * 512],
                       xob, q, waits=[(mv, ev)])
            xo_free[tt % 2] = (q, ov)
        wdsl_free = (mt, mt.val)
    L["region_free_t"] = (mt, mt.val)
    L["region_free_v"] = (mv, mv.val)
    L["region_free_s"] = (ms, ms.val)
    return (q, q.val)


def alloc_l1ffn_bufs(k, L):
    ar = k.ar
    G = {}
    base = L["ov1"]
    o = base
    G["ids"] = []
    for i in range(2):
        G["ids"].append(ar.place([128, INTER], BF16, o, f"gids{i}"))
        o += INTER * 2 + 32 - (INTER * 2) % 32
    G["eqt"] = ar.place([128, INTER], BF16, o, "geqt"); o += INTER * 2 + 16
    o = (o + 31) & ~31
    G["acc"] = []
    for i in range(2):
        G["acc"].append(ar.place([128, INTER], BF16, o, f"gacc{i}"))
        o += INTER * 2 + 16
        o = (o + 31) & ~31
    assert o <= L["ov1"] + L["ovsz"]
    L["uslot"] = [ar.alloc([128, 2048], BF16, f"usl{i}") for i in range(2)]
    mv = k.sem["m_vec"]
    L.setdefault("uslot_free", [(mv, 0), (mv, 0)])
    L.setdefault("silu_free", [(mv, 0), (mv, 0)])
    return G


def prep_l1ffn_inputs(inp):
    d = {}
    d["gidxT"] = np.ascontiguousarray(np.asarray(inp["L1_ffn_gate_idx"]).T)
    d["uidxT"] = np.ascontiguousarray(np.asarray(inp["L1_ffn_up_idx"]).T)
    d["didxT"] = np.ascontiguousarray(np.asarray(inp["L1_ffn_down_idx"]).T)
    d["lut"] = np.asarray(inp["L1_ffn_lut"], np.float32).reshape(1, 16)
    return d


SC1 = 1.0 / np.sqrt(192.0)


def setup_l1attn_weights(k):
    nc, P, ar = k.nc, k.P, k.ar
    q, qg = k.sem["q_sync"], k.sem["q_gp"]
    W = {}
    din = k.din
    for nm, shp in (("wkvd", [4 * 128, 2048]), ("wqd", [12 * 128, 2048]),
                    ("wku", [16 * 128, 512]), ("wvu", [16 * 128, 512]),
                    ("wkr", [16 * 128, 256]), ("wqu", [16 * 128, 1536]),
                    ("wqr", [16 * 128, 768]), ("wo1", [4 * 16 * 128, 512])):
        din[nm] = nc.dram_tensor(nm, shp, F32, kind="ExternalInput").ap()
    din["bkvd"] = nc.dram_tensor("bkvd", [128, 4], F32, kind="ExternalInput").ap()
    din["bqd"] = nc.dram_tensor("bqd", [128, 12], F32, kind="ExternalInput").ap()
    din["bku"] = nc.dram_tensor("bku", [128, 16], F32, kind="ExternalInput").ap()
    din["bqu"] = nc.dram_tensor("bqu", [128, 16], F32, kind="ExternalInput").ap()
    din["bkr"] = nc.dram_tensor("bkr", [64, 16], F32, kind="ExternalInput").ap()
    din["bqr"] = nc.dram_tensor("bqr", [64, 16], F32, kind="ExternalInput").ap()
    din["bvu"] = nc.dram_tensor("bvu", [1, 2048], F32, kind="ExternalInput").ap()
    din["bo1"] = nc.dram_tensor("bo1", [1, 2048], F32, kind="ExternalInput").ap()
    din["rope1"] = nc.dram_tensor("rope1", [64, 2 * 2048], BF16,
                                  kind="ExternalInput").ap()
    din["masks"] = nc.dram_tensor("masks", [128, NMASK * 512], BF16,
                                  kind="ExternalInput").ap()
    W["ctxsc"] = nc.dram_tensor("ctxsc1", [16 * 128, 2048], BF16,
                                kind="Internal").ap()
    for nm, shp in (("bkvd", [128, 4]), ("bqd", [128, 12]), ("bku", [128, 16]),
                    ("bqu", [128, 16]), ("bkr", [64, 16]), ("bqr", [64, 16])):
        W[nm] = ar.alloc(shp, F32, nm + "_s")
        v = P.dma("sync", W[nm], din[nm], q)
    W["rope1"] = ar.alloc([64, 2 * 2048], BF16, "rope1_s")
    W["masks"] = ar.alloc([128, NMASK * 512], BF16, "masks1_s")
    v = P.dma("sync", W["rope1"], din["rope1"], q)
    v = P.dma("sync", W["masks"], din["masks"], q)
    W["ready"] = (q, v)
    W["bvu"] = ar.alloc([1, 2048], BF16, "bvu_s")
    W["bo"] = ar.alloc([1, 2048], BF16, "bo1_s")
    v = P.dma("gpsimd", W["bvu"], din["bvu"], qg)
    v = P.dma("gpsimd", W["bo"], din["bo1"], qg)
    W["brow_ready"] = (qg, v)
    return W


def alloc_l1attn_bufs(k, L):
    ar = k.ar
    A = {}
    base = L["ov1"]
    o = base + NKD * T * 2
    A["kvc"] = ar.place([128, 4 * T], BF16, o, "kvc"); o += 4 * T * 2
    A["qc"] = ar.place([128, 12 * T], BF16, o, "qc"); o += 12 * T * 2
    assert o <= base + L["ovsz"], o
    o = base
    A["Kc"] = ar.place([128, T], BF16, o, "KcH"); o += T * 2
    A["Vh"] = ar.place([128, T], BF16, o, "VhH"); o += T * 2
    A["Qc"] = ar.place([128, T], BF16, o, "QcH"); o += T * 2
    A["Kr"] = ar.place([64, T], BF16, o, "KrH"); o += T * 2
    A["Qr"] = ar.place([64, T], BF16, o, "QrH"); o += T * 2
    A["km64"] = ar.place([64, T], BF16, o, "km64"); o += T * 2
    A["kms64"] = ar.place([64, T], BF16, o, "kms64"); o += T * 2
    A["rt64"] = ar.place([64, T], BF16, o, "rt64"); o += T * 2
    A["Pb"] = ar.place([128, T], BF16, o, "Pb1"); o += T * 2
    A["PTb"] = ar.place([128, T], BF16, o, "PTb1"); o += T * 2
    A["hw"] = ar.place([128, 3584], BF16, o, "hw1"); o += 3584 * 2
    kq_end = o
    o = base
    A["prw"] = ar.place([128, 16 * 512], BF16, o, "prw1"); o += 16 * 512 * 2
    A["prslot"] = []
    for i in range(2):
        A["prslot"].append(ar.place([128, 16 * 128], BF16, o, f"prslot1_{i}"))
        o += 16 * 128 * 2
    assert max(kq_end, o) <= base + NKD * T * 2
    A["crow"] = [ar.alloc([128, T], BF16, f"crow1_{i}") for i in range(2)]
    A["dwsl"] = [ar.alloc([128, 2048], BF16, f"dwsl{i}") for i in range(2)]
    return A


def emit_l1attn(k, xin_rows, xout_rows, W, L, A):
    P, C = k.P, k.C
    q, qg = k.sem["q_sync"], k.sem["q_gp"]
    mv, ms, mt = k.sem["m_vec"], k.sem["m_scl"], k.sem["m_tns"]
    XN = L["XN"]
    stats = C["stats"]

    P.wait_only("vector", [L["region_free_t"], L["region_free_s"]])
    P.wait_only("sync", [L["region_free_t"], L["region_free_v"],
                         L["region_free_s"]])
    A["kms_free"] = A.get("kms_free", (mv, 0))
    A["ropet_free"] = A.get("ropet_free", (mv, 0))

    xn_done = emit_ln_to_xn(k, xin_rows, XN, L)

    wslot = A["dwsl"]
    wslot_free = A.setdefault("dwsl_free", [(mv, 0), (mv, 0)])
    pf_free = L["pfA_free"]
    nslot = 0
    for dst, wnm, nct, bcol in ((A["kvc"], "wkvd", 4, W["bkvd"]),
                                (A["qc"], "wqd", 12, W["bqd"])):
        for ct in range(nct):
            ws = wslot[nslot % 2]
            wv = P.dma("gpsimd", ws, k.din[wnm][ct * 128:(ct + 1) * 128, :], qg,
                       waits=[wslot_free[nslot % 2], (mv, xn_done),
                              (mt, mt.val), (ms, ms.val)])
            for tc in range(NTCH):
                pf = k.PF[tc % 2]
                tv = 0
                for kt in range(NKD):
                    tv = P.op("tensor",
                              lambda e_, ws=ws, kt=kt, tc=tc, pf=pf: e_.matmul(
                                  pf[:, 0:512], ws[:, kt * 128:(kt + 1) * 128],
                                  XN[:, kt * T + tc * 512:kt * T + (tc + 1) * 512],
                                  start=(kt == 0), stop=(kt == NKD - 1)),
                              waits=([(qg, wv), pf_free[tc % 2]]
                                     if kt == 0 else ()),
                              inc=(mt if kt == NKD - 1 else None))
                vv = P.op("vector",
                          lambda e_, dst=dst, ct=ct, tc=tc, pf=pf, bcol=bcol:
                          e_.tensor_scalar(
                              dst[:, ct * T + tc * 512:ct * T + (tc + 1) * 512],
                              pf[:, 0:512], bcol[:, ct:ct + 1], None, ALU.add),
                          waits=[(mt, tv), W["ready"]], inc=mv)
                pf_free[tc % 2] = (mv, vv)
            wslot_free[nslot % 2] = (mt, tv)
            nslot += 1
    kvq_done = (mv, mv.val)

    att_bar = [(mt, mt.val), (mv, mv.val), (ms, ms.val)]
    hw_free = att_bar[0]
    pfS_free = [att_bar[1]] * 4
    pb_free = L["pb_free"]
    psc_free = L["psc_free"]
    crow_free = [(q, 0), (q, 0)]
    buf_free = att_bar[0]
    first = att_bar
    kvc, qc = A["kvc"], A["qc"]
    for h in range(16):
        hw = A["hw"]
        hv1 = P.dma("gpsimd", hw[:, 0:512], k.din["wku"][h * 128:(h + 1) * 128, :],
                    qg, waits=[hw_free] + first)
        first = []
        hv1 = P.dma("gpsimd", hw[:, 512:1024],
                    k.din["wvu"][h * 128:(h + 1) * 128, :], qg)
        hv1 = P.dma("gpsimd", hw[:, 1024:1280],
                    k.din["wkr"][h * 128:(h + 1) * 128, :], qg)
        hv1 = P.dma("gpsimd", hw[:, 1280:2816],
                    k.din["wqu"][h * 128:(h + 1) * 128, :], qg)
        hv1 = P.dma("gpsimd", hw[:, 2816:3584],
                    k.din["wqr"][h * 128:(h + 1) * 128, :], qg)
        hwv = (qg, hv1)
        for tc in range(NTCH):
            pf = k.PF[tc % 2]
            tv = 0
            for ct in range(4):
                tv = P.op("tensor",
                          lambda e_, hw=hw, ct=ct, tc=tc, pf=pf: e_.matmul(
                              pf[:, 0:512], hw[:, ct * 128:(ct + 1) * 128],
                              kvc[:, ct * T + tc * 512:ct * T + (tc + 1) * 512],
                              start=(ct == 0), stop=(ct == 3)),
                          waits=([hwv, pf_free[tc % 2], buf_free, kvq_done]
                                 if ct == 0 else ()),
                          inc=(mt if ct == 3 else None))
            vv = P.op("vector", lambda e_, tc=tc, pf=pf, h=h: e_.tensor_scalar(
                A["Kc"][:, tc * 512:(tc + 1) * 512], pf[:, 0:512],
                W["bku"][:, h:h + 1], None, ALU.add),
                waits=[(mt, tv)], inc=mv)
            pf_free[tc % 2] = (mv, vv)
        for tc in range(NTCH):
            pf = k.PF[tc % 2]
            tv = 0
            for ct in range(4):
                tv = P.op("tensor",
                          lambda e_, hw=hw, ct=ct, tc=tc, pf=pf: e_.matmul(
                              pf[0:64, 0:512],
                              hw[:, 1024 + ct * 64:1024 + (ct + 1) * 64],
                              kvc[:, ct * T + tc * 512:ct * T + (tc + 1) * 512],
                              start=(ct == 0), stop=(ct == 3)),
                          waits=([pf_free[tc % 2]] if ct == 0 else ()),
                          inc=(mt if ct == 3 else None))
            vv = P.op("vector", lambda e_, tc=tc, pf=pf, h=h: e_.tensor_scalar(
                A["km64"][:, tc * 512:(tc + 1) * 512], pf[0:64, 0:512],
                W["bkr"][:, h:h + 1], None, ALU.add),
                waits=[(mt, tv), A["kms_free"]], inc=mv)
            pf_free[tc % 2] = (mv, vv)
        vv = emit_rope_head(k, A, W["rope1"], vv, A["Kr"][0:64, :], half=32,
                            km=A["km64"], kms=A["kms64"], rt=A["rt64"])
        for tt in range(NT):
            pf = k.PF[2 + tt % 2]
            tv = 0
            for ct in range(4):
                tv = P.op("tensor",
                          lambda e_, hw=hw, ct=ct, tt=tt, pf=pf: e_.matmul(
                              pf[:, 0:128],
                              kvc[:, ct * T + tt * 128:ct * T + (tt + 1) * 128],
                              hw[:, 512 + ct * 128:512 + (ct + 1) * 128],
                              start=(ct == 0), stop=False),
                          waits=([L["pfB_free"][tt % 2]] if ct == 0 else ()))
            tv = P.op("tensor", lambda e_, pf=pf, h=h: e_.matmul(
                pf[:, 0:128], C["ones_row"][0:1, 0:128],
                W["bvu"][0:1, h * 128:(h + 1) * 128], start=False, stop=True),
                waits=[W["brow_ready"], C["cvec"]], inc=mt)
            vv = P.op("vector", lambda e_, tt=tt, pf=pf: e_.tensor_copy(
                A["Vh"][:, tt * 128:(tt + 1) * 128], pf[:, 0:128]),
                waits=[(mt, tv)], inc=mv)
            L["pfB_free"][tt % 2] = (mv, vv)
        for tc in range(NTCH):
            pf = k.PF[tc % 2]
            tv = 0
            for ct in range(12):
                tv = P.op("tensor",
                          lambda e_, hw=hw, ct=ct, tc=tc, pf=pf: e_.matmul(
                              pf[:, 0:512],
                              hw[:, 1280 + ct * 128:1280 + (ct + 1) * 128],
                              qc[:, ct * T + tc * 512:ct * T + (tc + 1) * 512],
                              start=(ct == 0), stop=(ct == 11)),
                          waits=([pf_free[tc % 2]] if ct == 0 else ()),
                          inc=(mt if ct == 11 else None))
            vv = P.op("vector", lambda e_, tc=tc, pf=pf, h=h: e_.tensor_scalar(
                A["Qc"][:, tc * 512:(tc + 1) * 512], pf[:, 0:512],
                W["bqu"][:, h:h + 1], None, ALU.add),
                waits=[(mt, tv)], inc=mv)
            pf_free[tc % 2] = (mv, vv)
        for tc in range(NTCH):
            pf = k.PF[tc % 2]
            tv = 0
            for ct in range(12):
                tv = P.op("tensor",
                          lambda e_, hw=hw, ct=ct, tc=tc, pf=pf: e_.matmul(
                              pf[0:64, 0:512],
                              hw[:, 2816 + ct * 64:2816 + (ct + 1) * 64],
                              qc[:, ct * T + tc * 512:ct * T + (tc + 1) * 512],
                              start=(ct == 0), stop=(ct == 11)),
                          waits=([pf_free[tc % 2]] if ct == 0 else ()),
                          inc=(mt if ct == 11 else None))
            vv = P.op("vector", lambda e_, tc=tc, pf=pf, h=h: e_.tensor_scalar(
                A["km64"][:, tc * 512:(tc + 1) * 512], pf[0:64, 0:512],
                W["bqr"][:, h:h + 1], None, ALU.add),
                waits=[(mt, tv), A["kms_free"]], inc=mv)
            pf_free[tc % 2] = (mv, vv)
        qr_done = emit_rope_head(k, A, W["rope1"], vv, A["Qr"][0:64, :], half=32,
                                 km=A["km64"], kms=A["kms64"], rt=A["rt64"])
        proj_done = (mv, qr_done)

        for qt in range(NT):
            nch = qt // 4 + 1
            stv = []
            for cc in range(nch):
                P.op("tensor", lambda e_, qt=qt, cc=cc: e_.matmul(
                    k.PF[cc][:, 0:512], A["Qc"][:, qt * 128:(qt + 1) * 128],
                    A["Kc"][:, cc * 512:(cc + 1) * 512],
                    start=True, stop=False),
                    waits=[proj_done, pfS_free[cc]])
                tv = P.op("tensor", lambda e_, qt=qt, cc=cc: e_.matmul(
                    k.PF[cc][:, 0:512], A["Qr"][0:64, qt * 128:(qt + 1) * 128],
                    A["Kr"][0:64, cc * 512:(cc + 1) * 512],
                    start=False, stop=True), inc=mt)
                stv.append(tv)
            mvv = P.op("vector", lambda e_, qt=qt, nch=nch: e_.tensor_tensor(
                k.PF[nch - 1][:, 0:512], k.PF[nch - 1][:, 0:512],
                W["masks"][:, (qt % 4) * 512:(qt % 4 + 1) * 512], ALU.add),
                waits=[(mt, stv[-1]), W["ready"]], inc=mv)
            vv = P.op("vector", lambda e_: e_.reduce_max(
                stats[:, 16:17], k.PF[0][:, 0:512], axis=AX.X),
                waits=[(mt, stv[0]), ((mv, mvv) if nch == 1 else (mv, 0))],
                inc=mv)
            for cc in range(1, nch):
                vv = P.op("vector", lambda e_, cc=cc: e_.reduce_max(
                    stats[:, 17:18], k.PF[cc][:, 0:512], axis=AX.X),
                    waits=[(mt, stv[cc]), (mv, mvv), (mv, vv)], inc=mv)
                vv = P.op("vector", lambda e_: e_.tensor_tensor(
                    stats[:, 16:17], stats[:, 16:17], stats[:, 17:18], ALU.max),
                    waits=[(mv, vv)], inc=mv)
            vv = P.op("vector", lambda e_: e_.tensor_scalar(
                stats[:, 18:19], stats[:, 16:17], -SC1, None, ALU.mult),
                waits=[(mv, vv)], inc=mv)
            exv = []
            for cc in range(nch):
                sv = P.op("scalar", lambda e_, cc=cc: e_.activation(
                    A["Pb"][:, cc * 512:(cc + 1) * 512], k.PF[cc][:, 0:512],
                    AF.Exp, scale=SC1, bias=stats[:, 18:19],
                    accum_out=stats[:, 20 + cc:21 + cc]),
                    waits=[(mv, vv)], inc=ms)
                exv.append(sv)
                pfS_free[cc] = (ms, sv)
            vv = P.op("vector", lambda e_: e_.tensor_copy(
                stats[:, 24:25], stats[:, 20:21]), waits=[(ms, exv[0])], inc=mv)
            for cc in range(1, nch):
                vv = P.op("vector", lambda e_, cc=cc: e_.tensor_tensor(
                    stats[:, 24:25], stats[:, 24:25], stats[:, 20 + cc:21 + cc],
                    ALU.add), waits=[(ms, exv[cc]), (mv, vv)], inc=mv)
            vv = P.op("vector", lambda e_: e_.reciprocal(
                stats[:, 25:26], stats[:, 24:25]), waits=[(mv, vv)], inc=mv)
            scv = []
            for cc in range(nch):
                v2 = P.op("vector", lambda e_, cc=cc: e_.tensor_scalar(
                    A["Pb"][:, cc * 512:(cc + 1) * 512],
                    A["Pb"][:, cc * 512:(cc + 1) * 512],
                    stats[:, 25:26], None, ALU.mult), waits=[(mv, vv)], inc=mv)
                scv.append(v2)
            nkt = nch * 4
            ptv = []
            for grp in range((nkt + 3) // 4):
                pb = k.PB[grp % 2]
                tv = 0
                for j in range(4):
                    kt = grp * 4 + j
                    if kt >= nkt:
                        break
                    tv = P.op("tensor", lambda e_, kt=kt, j=j, pb=pb: e_.transpose(
                        pb[:, j * 128:(j + 1) * 128],
                        A["Pb"][:, kt * 128:(kt + 1) * 128], C["idn_bf"]),
                        waits=([(mv, scv[-1]), pb_free[grp % 2]]
                               if j == 0 else ()),
                        inc=(mt if (j == 3 or kt == nkt - 1) else None))
                nb = min(4, nkt - grp * 4)
                ev = P.op("vector", lambda e_, pb=pb, grp=grp, nb=nb: e_.tensor_copy(
                    A["PTb"][:, grp * 512:grp * 512 + nb * 128],
                    pb[:, 0:nb * 128]), waits=[(mt, tv)], inc=mv)
                pb_free[grp % 2] = (mv, ev)
                ptv.append(ev)
            tv = 0
            for kt in range(nkt):
                tv = P.op("tensor", lambda e_, kt=kt, nkt=nkt: e_.matmul(
                    k.PSC[:, 0:128], A["Vh"][:, kt * 128:(kt + 1) * 128],
                    A["PTb"][:, kt * 128:(kt + 1) * 128],
                    start=(kt == 0), stop=(kt == nkt - 1)),
                    waits=([(mv, ptv[-1]), psc_free] if kt == 0 else ()),
                    inc=(mt if kt == nkt - 1 else None))
            cv = P.op("vector", lambda e_, qt=qt, h=h: e_.tensor_copy(
                A["crow"][h % 2][:, qt * 128:(qt + 1) * 128], k.PSC[:, 0:128]),
                waits=[(mt, tv), crow_free[h % 2]], inc=mv)
            psc_free = (mv, cv)
        dv = P.dma("sync", W["ctxsc"][h * 128:(h + 1) * 128, :],
                   A["crow"][h % 2][:], q, waits=[(mv, cv)])
        crow_free[h % 2] = (q, dv)
        hw_free = (mt, mt.val)
        buf_free = (mt, mt.val)
    ctx_done = (q, dv)
    L["psc_free"] = psc_free

    op_bar = [(mt, mt.val), (mv, mv.val), (ms, ms.val)]
    prslot_free = [op_bar[0], op_bar[0]]
    xs, xo = L["xs"], L["xo"]
    xs_free, xo_free, pfB_free = L["xs_free"], L["xo_free"], L["pfB_free"]
    first = op_bar
    prw_free = op_bar[0]
    for dc in range(4):
        wv = P.dma("gpsimd", A["prw"][:].rearrange("p (r m) -> p r m", r=16),
                   k.din["wo1"][dc * 16 * 128:(dc + 1) * 16 * 128, :].rearrange(
                       "(r p) m -> p r m", p=128),
                   qg, waits=[prw_free] + first)
        first = []
        for tt in range(NT):
            t0 = tt * 128
            ps = A["prslot"][tt % 2]
            pv = P.dma("sync", ps[:].rearrange("p (r m) -> p r m", r=16),
                       W["ctxsc"].rearrange("(r p) m -> p r m", p=128)
                       [:, :, t0:t0 + 128],
                       q, waits=[prslot_free[tt % 2], ctx_done])
            xsb = xs[tt % 3]
            xv = P.dma("sync", xsb,
                       xin_rows[t0:t0 + 128, dc * 512:(dc + 1) * 512],
                       q, waits=[xs_free[tt % 3]])
            pf = k.PF[2 + tt % 2]
            P.op("tensor", lambda e_, pf=pf, dc=dc: e_.matmul(
                pf[:, 0:512], C["ones_row"][0:1, 0:128],
                W["bo"][0:1, dc * 512:(dc + 1) * 512], start=True, stop=False),
                waits=[pfB_free[tt % 2], W["brow_ready"], (qg, wv), (q, pv)])
            tv = 0
            for ct in range(16):
                tv = P.op("tensor", lambda e_, ps=ps, ct=ct, pf=pf: e_.matmul(
                    pf[:, 0:512], ps[:, ct * 128:(ct + 1) * 128],
                    A["prw"][:, ct * 512:(ct + 1) * 512],
                    start=False, stop=(ct == 15)),
                    inc=(mt if ct == 15 else None))
            prslot_free[tt % 2] = (mt, tv)
            xob = xo[tt % 2]
            ev = P.op("vector", lambda e_, xob=xob, pf=pf, xsb=xsb: e_.tensor_tensor(
                xob, pf[:, 0:512], xsb, ALU.add),
                waits=[(mt, tv), (q, xv), xo_free[tt % 2]], inc=mv)
            pfB_free[tt % 2] = (mv, ev)
            xs_free[tt % 3] = (mv, ev)
            ov = P.dma("sync", xout_rows[t0:t0 + 128, dc * 512:(dc + 1) * 512],
                       xob, q, waits=[(mv, ev)])
            xo_free[tt % 2] = (q, ov)
        prw_free = (mt, mt.val)
    L["pfA_free"][0] = (mv, mv.val)
    L["pfA_free"][1] = (mv, mv.val)
    L["pfB_free"][0] = (mv, mv.val)
    L["pfB_free"][1] = (mv, mv.val)
    L["region_free_t"] = (mt, mt.val)
    L["region_free_v"] = (mv, mv.val)
    L["region_free_s"] = (ms, ms.val)


def prep_l1attn_inputs(inp):
    d = {}
    g = lambda n: np.asarray(inp["L1_attn_" + n], np.float32)
    d["wkvd"] = np.ascontiguousarray(
        g("W_kvd").reshape(16, 128, 4, 128).transpose(2, 1, 0, 3)).reshape(512, 2048)
    d["wqd"] = np.ascontiguousarray(
        g("W_qd").reshape(16, 128, 12, 128).transpose(2, 1, 0, 3)).reshape(1536, 2048)
    d["wku"] = np.ascontiguousarray(
        g("W_ku").reshape(4, 128, 16, 128).transpose(2, 1, 0, 3)).reshape(2048, 512)
    d["wvu"] = np.ascontiguousarray(
        g("W_vu").reshape(4, 128, 16, 128).transpose(2, 1, 0, 3)).reshape(2048, 512)
    d["wkr"] = np.ascontiguousarray(
        g("W_kr").reshape(4, 128, 16, 64).transpose(2, 1, 0, 3)).reshape(2048, 256)
    d["wqu"] = np.ascontiguousarray(
        g("W_qu").reshape(12, 128, 16, 128).transpose(2, 1, 0, 3)).reshape(2048, 1536)
    d["wqr"] = np.ascontiguousarray(
        g("W_qr").reshape(12, 128, 16, 64).transpose(2, 1, 0, 3)).reshape(2048, 768)
    d["wo1"] = np.ascontiguousarray(
        g("W_o").reshape(16, 128, 4, 512).transpose(2, 0, 1, 3)).reshape(8192, 512)
    d["bkvd"] = np.ascontiguousarray(g("b_kvd").reshape(4, 128).T)
    d["bqd"] = np.ascontiguousarray(g("b_qd").reshape(12, 128).T)
    d["bku"] = np.ascontiguousarray(g("b_ku").reshape(16, 128).T)
    d["bqu"] = np.ascontiguousarray(g("b_qu").reshape(16, 128).T)
    d["bkr"] = np.ascontiguousarray(g("b_kr").reshape(16, 64).T)
    d["bqr"] = np.ascontiguousarray(g("b_qr").reshape(16, 64).T)
    d["bvu"] = g("b_vu").reshape(1, 2048)
    d["bo1"] = g("b_o").reshape(1, 2048)
    inv = 1.0 / 10000.0 ** (np.arange(0, 64, 2) / 64.0)
    ang = np.arange(T)[:, None] * inv[None, :]
    cosT, sinT = np.cos(ang).T, np.sin(ang).T
    Cf = np.concatenate([cosT, cosT], 0)
    Sx = np.concatenate([-sinT, sinT], 0)
    d["rope1"] = np.concatenate([Cf, Sx], 1).astype(BF)
    m = np.zeros((128, NMASK, 512), np.float32)
    for v in range(NMASK):
        r = np.arange(128)[:, None]
        kk = np.arange(512)[None, :]
        m[:, v, :] = np.where(kk <= v * 128 + r, 0.0, -1e9)
    d["masks"] = m.reshape(128, NMASK * 512).astype(BF)
    return d

_prog_cache = {}


def build_l0ffn_program():
    nc = bass.Bass("TRN2", target_bir_lowering=False, debug=False)
    ctx = ExitStack()
    with ctx:
        k = K(nc, ctx)
        xin = nc.dram_tensor("x_in", [NSEQ * T, D], F32, kind="ExternalInput").ap()
        xout = nc.dram_tensor("x_out", [NSEQ * T, D], F32,
                              kind="ExternalOutput").ap()
        k.din["idn_bf"] = nc.dram_tensor("idn_bf", [128, 128], BF16,
                                         kind="ExternalInput").ap()
        k.din["idn_f32"] = nc.dram_tensor("idn_f32", [128, 128], F32,
                                          kind="ExternalInput").ap()
        setup_consts(k)
        W = setup_l0ffn_weights(k)
        L = alloc_shared_bufs(k)
        for seq in range(NSEQ):
            emit_l0ffn(k, xin[seq * T:(seq + 1) * T, :],
                       xout[seq * T:(seq + 1) * T, :], W, L)
        k.P.wait_only("sync", [(k.sem["q_sync"], k.sem["q_sync"].val)])
        k.finalize()
    return nc


def build_l0attn_program():
    nc = bass.Bass("TRN2", target_bir_lowering=False, debug=False)
    ctx = ExitStack()
    with ctx:
        k = K(nc, ctx)
        xin = nc.dram_tensor("x_in", [NSEQ * T, D], F32, kind="ExternalInput").ap()
        xout = nc.dram_tensor("x_out", [NSEQ * T, D], F32,
                              kind="ExternalOutput").ap()
        k.dbg = nc.dram_tensor("dbg", [128, 32768], BF16,
                               kind="ExternalOutput").ap()
        k.din["idn_bf"] = nc.dram_tensor("idn_bf", [128, 128], BF16,
                                         kind="ExternalInput").ap()
        k.din["idn_f32"] = nc.dram_tensor("idn_f32", [128, 128], F32,
                                          kind="ExternalInput").ap()
        setup_consts(k)
        W = setup_l0attn_weights(k)
        L = alloc_shared_bufs(k)
        A = alloc_l0attn_bufs(k, L)
        nseq = NSEQ if ATT_PHASES >= 5 else 1
        for seq in range(nseq):
            emit_l0attn(k, xin[seq * T:(seq + 1) * T, :],
                        xout[seq * T:(seq + 1) * T, :], W, L, A)
        k.P.wait_only("sync", [(k.sem["q_sync"], k.sem["q_sync"].val)])
        k.finalize()
    return nc


def _run_sublayer(prog_name, build_fn, prep_fn, x_full, inp):
    if prog_name not in _prog_cache:
        _prog_cache[prog_name] = build_fn()
    nc = _prog_cache[prog_name]
    wd = prep_fn(inp)
    wd["idn_bf"] = np.eye(128, dtype=BF)
    wd["idn_f32"] = np.eye(128, dtype=np.float32)
    in_maps = []
    for c in range(NCORES):
        m = dict(wd)
        m["x_in"] = np.ascontiguousarray(
            x_full[2 * c:2 * c + 2].reshape(NSEQ * T, D))
        in_maps.append(m)
    res = run_bass_kernel_spmd(nc, in_maps, core_ids=list(range(NCORES)))
    out = np.stack([r["x_out"].reshape(NSEQ, T, D) for r in res.results])
    return out.reshape(16, T, D)


def build_l1ffn_program():
    nc = bass.Bass("TRN2", target_bir_lowering=False, debug=False)
    ctx = ExitStack()
    with ctx:
        k = K(nc, ctx)
        xin = nc.dram_tensor("x_in", [NSEQ * T, D], F32, kind="ExternalInput").ap()
        xout = nc.dram_tensor("x_out", [NSEQ * T, D], F32,
                              kind="ExternalOutput").ap()
        k.din["idn_bf"] = nc.dram_tensor("idn_bf", [128, 128], BF16,
                                         kind="ExternalInput").ap()
        k.din["idn_f32"] = nc.dram_tensor("idn_f32", [128, 128], F32,
                                          kind="ExternalInput").ap()
        setup_consts(k)
        W = setup_l1ffn_weights(k)
        L = alloc_shared_bufs(k)
        G = alloc_l1ffn_bufs(k, L)
        mat_done = emit_ghost_mat(k, W, G)
        mv, ms, mt = k.sem["m_vec"], k.sem["m_scl"], k.sem["m_tns"]
        L["region_free_t"] = (mt, mt.val)
        L["region_free_v"] = (mv, mv.val)
        L["region_free_s"] = (ms, ms.val)
        for seq in range(NSEQ):
            emit_l1ffn(k, xin[seq * T:(seq + 1) * T, :],
                       xout[seq * T:(seq + 1) * T, :], W, L, mat_done)
        k.P.wait_only("sync", [(k.sem["q_sync"], k.sem["q_sync"].val)])
        k.finalize()
    return nc


def build_l1attn_program():
    nc = bass.Bass("TRN2", target_bir_lowering=False, debug=False)
    ctx = ExitStack()
    with ctx:
        k = K(nc, ctx)
        xin = nc.dram_tensor("x_in", [NSEQ * T, D], F32, kind="ExternalInput").ap()
        xout = nc.dram_tensor("x_out", [NSEQ * T, D], F32,
                              kind="ExternalOutput").ap()
        k.din["idn_bf"] = nc.dram_tensor("idn_bf", [128, 128], BF16,
                                         kind="ExternalInput").ap()
        k.din["idn_f32"] = nc.dram_tensor("idn_f32", [128, 128], F32,
                                          kind="ExternalInput").ap()
        setup_consts(k)
        W = setup_l1attn_weights(k)
        L = alloc_shared_bufs(k)
        A = alloc_l1attn_bufs(k, L)
        for seq in range(NSEQ):
            emit_l1attn(k, xin[seq * T:(seq + 1) * T, :],
                        xout[seq * T:(seq + 1) * T, :], W, L, A)
        k.P.wait_only("sync", [(k.sem["q_sync"], k.sem["q_sync"].val)])
        k.finalize()
    return nc


def _run_l0ffn(x_full, inp):
    if "l0ffn" not in _prog_cache:
        _prog_cache["l0ffn"] = build_l0ffn_program()
    nc = _prog_cache["l0ffn"]
    wd = prep_l0ffn_inputs(inp)
    in_maps = []
    for c in range(NCORES):
        m = dict(wd)
        m["x_in"] = np.ascontiguousarray(
            x_full[2 * c:2 * c + 2].reshape(NSEQ * T, D))
        in_maps.append(m)
    res = run_bass_kernel_spmd(nc, in_maps, core_ids=list(range(NCORES)))
    out = np.stack([r["x_out"].reshape(NSEQ, T, D) for r in res.results])
    return out.reshape(16, T, D)


N_SUBLAYERS = 6


def kernel(**inputs):
    x = np.asarray(inputs["L0_attn_x"], np.float32)
    x = _run_l0ffn(x, inputs)
    if N_SUBLAYERS >= 2:
        x = _run_sublayer("l0attn", build_l0attn_program, prep_l0attn_inputs,
                          x, inputs)
    if N_SUBLAYERS >= 3:
        x = _run_l0ffn(x, inputs)
    if N_SUBLAYERS >= 4:
        x = _run_sublayer("l1ffn", build_l1ffn_program, prep_l1ffn_inputs,
                          x, inputs)
    if N_SUBLAYERS >= 5:
        x = _run_sublayer("l1attn", build_l1attn_program, prep_l1attn_inputs,
                          x, inputs)
    if N_SUBLAYERS >= 6:
        x = _run_sublayer("l1ffn", build_l1ffn_program, prep_l1ffn_inputs,
                          x, inputs)
    return x
